# version 08
# baseline (speedup 1.0000x reference)
import collections
import numpy as np
import ml_dtypes
from contextlib import ExitStack

import concourse.bass as bass
import concourse.mybir as mybir
from concourse.bass_utils import run_bass_kernel_spmd

F32 = mybir.dt.float32
BF16 = mybir.dt.bfloat16
I32 = mybir.dt.int32
AF = mybir.ActivationFunctionType
ALU = mybir.AluOpType
AX = mybir.AxisListType

T = 2048
D = 1024
NKC = D // 128
LAT = 512
NLC = LAT // 128
H = 16
HD = 64
FF = 4096
NFC = FF // 128
SEG = 512
NSEG = T // SEG
NT = T // 128
KB = 1024
HK = KB // 2

ENGS = ("sync", "tensor", "vector", "scalar", "gpsimd")


class Builder:
    def __init__(self, debug=False):
        self.debug = debug
        self.nc = bass.Bass("TRN2", target_bir_lowering=False,
                            detect_race_conditions=False)
        self.q = {e: [] for e in ENGS}
        self.cnt = {}
        self.handles = {}
        for s in ("dx", "dw", "dwf", "dscr", "dres", "dffn", "dout",
                  "pe", "act", "dve", "pool"):
            self.cnt[s] = 0

    def inc(self, eng, sem, n, build):
        self.cnt[sem] += n

        def f(e, build=build, sem=sem, n=n):
            build(e).then_inc(self.handles[sem], n)

        self.q[eng].append(f)
        return self.cnt[sem]

    def dma_inc(self, eng, sem, build):
        self.wait(eng, sem)
        return self.inc(eng, sem, 16, build)

    def drain_inc(self, eng, sem):
        self.cnt[sem] += 1

        def f(e, sem=sem):
            e.maybe_drain_then_inc((self.handles[sem], 1), fusable=False)

        self.q[eng].append(f)
        return self.cnt[sem]

    def wait(self, eng, sem, val=None):
        v = self.cnt[sem] if val is None else val
        if v <= 0:
            return
        assert v <= self.cnt[sem], \
            f"wait on future {sem} value {v} > planned {self.cnt[sem]}"

        def f(e, sem=sem, v=v):
            e.wait_ge(self.handles[sem], v)

        self.q[eng].append(f)

    def ck(self, sem):
        return self.cnt[sem]

    def mm(self, out, lhsT, rhs, start=True, stop=True, tp=None):
        def f(e, out=out, lhsT=lhsT, rhs=rhs, start=start, stop=stop, tp=tp):
            return e.matmul(out, lhsT, rhs, start=start, stop=stop,
                            tile_position=tp, skip_group_check=True)
        return f

    def alloc(self):
        nc = self.nc
        self.din = {}
        for name, shape, dt in [
            ("xT", [D, T], F32),
            ("wcd", [D, LAT], F32),
            ("wqup", [LAT, D], F32),
            ("wkv", [LAT, 2 * HD], F32),
            ("wcproj", [D, D], F32),
            ("wfcT", [D, FF], F32),
            ("wpjT", [FF, D], F32),
            ("afcT", [D, 16], F32),
            ("bfcT", [16, FF], F32),
            ("apjT", [FF, 16], F32),
            ("bpjT", [16, D], F32),
            ("bfc2d", [128, NFC], F32),
            ("bpj2d", [128, NKC], F32),
            ("ident", [128, 128], BF16),
            ("tril", [128, 128], BF16),
        ]:
            self.din[name] = nc.dram_tensor(name, shape, dt,
                                            kind="ExternalInput").ap()
        self.outT = nc.dram_tensor("outT", [D, T], F32,
                                   kind="ExternalOutput").ap()
        self.scr_fc = nc.dram_tensor("scr_fc", [8, 128, 8, 512], BF16,
                                     kind="Internal").ap()
        self.scr_pj = nc.dram_tensor("scr_pj", [2, 128, 32, 512], BF16,
                                     kind="Internal").ap()
        self.scr_acc = nc.dram_tensor("scr_acc", [2, 128], F32,
                                      kind="Internal").ap()
        self.dbg = {}
        if self.debug:
            for name, shape, dt in [
                ("d_qT", [128, NKC * T], BF16),
                ("d_kvT", [128, T], BF16),
                ("d_negc", [128, 1], F32),
                ("d_wfc", [8, 128, 8, 512], BF16),
                ("d_wpj", [2, 128, 32, 512], BF16),
                ("d_ytg", [128, NKC * SEG], BF16),
                ("d_x1g0", [128, NKC * SEG], F32),
                ("d_x1g1", [128, NKC * SEG], F32),
                ("d_xn2", [128, NKC * SEG], BF16),
                ("d_hT", [128, NFC * SEG], BF16),
                ("d_rt", [16, 2 * SEG], BF16),
                ("d_rnbf", [1, T], BF16),
                ("d_const", [128, 1024], F32),
                ("d_acc", [1, 256], F32),
            ]:
                self.dbg[name] = nc.dram_tensor(name, shape, dt,
                                                kind="ExternalOutput").ap()

        self.arena = nc.alloc_sbuf_tensor("arena", [128, 206 * KB],
                                          mybir.dt.uint8)[:, :]
        self.PB = nc.alloc_psum_tensor("pb", [128, 4096], F32)[:, :]

        O = {}
        off = 0

        def put(name, size):
            nonlocal off
            O[name] = off
            off += size
        put("CONST", 4096)
        put("QT", 32768)
        put("KVT", 4096)
        put("K2", 4096)
        put("VEXT", 2176)
        put("ATT", 4096)
        put("YTG", 8192)
        put("X1G", 32768)
        put("XN2G", 8192)
        put("HT", 32768)
        put("WFC_D", 16384)
        put("WPJ_D", 8192)
        put("P2Q", 12288)
        put("LORA", 11776)
        put("WCP_S", 4096)
        put("XRES", 4096)
        put("STATS", 16384)
        assert off <= 206 * KB, off
        a0 = O["YTG"]
        for name, size in [("XSEG", 32768), ("XNB", 16384), ("LATT", 16384),
                           ("WCD", 8192), ("WQUP", 8192), ("WKV", 1024),
                           ("MISC", 8192), ("P1", 16384)]:
            O[name] = a0
            a0 += size
        assert a0 <= O["LORA"], (a0, O["LORA"])
        O["RNB"] = O["K2"]
        O["RT1B"] = O["K2"] + 2048
        O["RT2B"] = O["K2"] + 3072
        self.O = O

    def view(self, off, dt, shape, part0=0, nparts=128):
        esz = {F32: 4, BF16: 2, I32: 4}[dt]
        n = 1
        for s in shape:
            n *= s
        v = self.arena[part0:part0 + nparts, off:off + n * esz].bitcast(dt)
        if len(shape) > 1:
            dims = " ".join(f"d{i}" for i in range(len(shape)))
            kw = {f"d{i}": shape[i] for i in range(1, len(shape))}
            v = v.rearrange(f"p ({dims}) -> p {dims}", **kw)
        return v

    def pbank(self, b, w=512):
        return self.PB[:, b * 512: b * 512 + w]

    def build(self):
        self.alloc()
        O, v = self.O, self.view
        self.c_ident = v(O["CONST"], BF16, [128])
        self.c_tril = v(O["CONST"] + 256, BF16, [128])
        self.c_ones_col = v(O["CONST"] + 512, BF16, [1])
        self.c_ones_row = v(O["CONST"] + 576, BF16, [128], nparts=1)
        self.c_ones_row_f = v(O["CONST"] + 832, F32, [128], nparts=1)
        self.c_bfc = v(O["CONST"] + 1408, F32, [NFC])
        self.c_bpj = v(O["CONST"] + 1600, F32, [NKC])
        self.c_negc = v(O["CONST"] + 1664, F32, [1])
        self.c_rscale_fc = v(O["CONST"] + 1728, F32, [1])
        self.c_scale_fc = v(O["CONST"] + 1792, F32, [1])
        self.c_rscale_pj = v(O["CONST"] + 1856, F32, [1])
        self.c_scale_pj = v(O["CONST"] + 1920, F32, [1])
        self.c_eps = v(O["CONST"] + 1984, F32, [1])
        self.c_nsc_fc = v(O["CONST"] + 2112, F32, [1])
        self.c_nsc_pj = v(O["CONST"] + 2240, F32, [1])
        self.qT = v(O["QT"], BF16, [NKC, T])
        self.kvT = v(O["KVT"], BF16, [T])
        self.k2 = v(O["K2"], BF16, [T], part0=64, nparts=64)
        self.vext = v(O["VEXT"], BF16, [NT, 65])
        self.latT = v(O["LATT"], BF16, [NLC, T])

        self.phase_A()
        self.phase_main()
        self.wait("sync", "dout")

        with ExitStack() as bctx:
            for s in list(self.cnt.keys()):
                self.handles[s] = bctx.enter_context(self.nc.semaphore(s))
            block = bctx.enter_context(self.nc.Block())
            for eng in ENGS:
                def body(e, eng=eng):
                    for fn in self.q[eng]:
                        fn(e)
                getattr(block, eng)(body)
        return self.nc

    def emit_layernorm_seg(self, src, dst, banks, pre_cast_wait=None):
        P, v = self, self.view
        so = self.O["STATS"]
        m_bf = v(so, BF16, [SEG], nparts=1)
        m_f = v(so + KB, F32, [SEG], nparts=1)
        msq = v(so + 3 * KB, F32, [SEG], nparts=1)
        var = v(so + 5 * KB, F32, [SEG], nparts=1)
        stdr = v(so + 7 * KB, F32, [SEG], nparts=1)
        rstd_bf = v(so + 9 * KB, BF16, [SEG], nparts=1)
        xb = [v(so + 10 * KB + b * KB, BF16, [SEG]) for b in range(2)]
        xsq = [v(so + 12 * KB + b * KB, BF16, [SEG]) for b in range(2)]
        tscr = v(so + 14 * KB, F32, [SEG])
        b_sum, b_ssq, b_m, b_r = banks
        ps_sum = self.PB[0:1, b_sum * 512:b_sum * 512 + SEG]
        ps_ssq = self.PB[0:1, b_ssq * 512:b_ssq * 512 + SEG]
        psB_m = self.pbank(b_m)
        psB_r = self.pbank(b_r)
        if pre_cast_wait is not None:
            pre_cast_wait()
        for kc in range(NKC):
            b = kc % 2
            if kc >= 2:
                P.wait("vector", "pe")
                P.wait("scalar", "pe")
            P.inc("vector", "dve", 1, lambda e, kc=kc, b=b: e.tensor_copy(
                xb[b], src[kc]))
            P.inc("scalar", "act", 1, lambda e, kc=kc, b=b: e.activation(
                xsq[b], src[kc], AF.Square))
            P.wait("tensor", "dve")
            P.wait("tensor", "act")
            P.inc("tensor", "pe", 1, P.mm(ps_sum, self.c_ones_col, xb[b],
                                          start=(kc == 0), stop=(kc == NKC - 1)))
            P.inc("tensor", "pe", 1, P.mm(ps_ssq, self.c_ones_col, xsq[b],
                                          start=(kc == 0), stop=(kc == NKC - 1)))
        P.wait("vector", "pe")
        P.inc("vector", "dve", 1, lambda e: e.tensor_scalar(
            m_f, ps_sum, 1.0 / D, None, ALU.mult))
        P.wait("vector", "dve")
        P.inc("vector", "dve", 1, lambda e: e.tensor_copy(m_bf, m_f))
        P.inc("vector", "dve", 1, lambda e: e.tensor_mul(msq, m_f, m_f))
        P.wait("vector", "dve")
        P.inc("vector", "dve", 1, lambda e: e.scalar_tensor_tensor(
            var, ps_ssq, 1.0 / D, msq, ALU.mult, ALU.subtract))
        P.wait("scalar", "dve")
        P.inc("scalar", "act", 1, lambda e: e.activation(
            stdr, var, AF.Ln, bias=self.c_eps[0:1, 0:1]))
        P.wait("scalar", "act")
        P.inc("scalar", "act", 1, lambda e: e.activation(
            rstd_bf, stdr, AF.Exp, scale=-0.5))
        P.wait("tensor", "dve")
        P.wait("tensor", "act")
        P.inc("tensor", "pe", 1, P.mm(psB_m, self.c_ones_row, m_bf))
        P.inc("tensor", "pe", 1, P.mm(psB_r, self.c_ones_row, rstd_bf))
        P.wait("vector", "pe")
        for kc in range(NKC):
            P.inc("vector", "dve", 1, lambda e, kc=kc: e.tensor_sub(
                tscr, src[kc], psB_m))
            P.wait("vector", "dve")
            P.inc("vector", "dve", 1, lambda e, kc=kc: e.tensor_mul(
                dst[kc], tscr, psB_r))

    def phase_A(self):
        P, v, O, din = self, self.view, self.O, self.din
        self.canb = v(O["CONST"] + 2304, BF16, [32])
        P.inc("sync", "dx", 16, lambda e: e.dma_start(self.c_ident, din["ident"]))
        P.inc("sync", "dx", 16, lambda e: e.dma_start(self.c_tril, din["tril"]))
        P.inc("sync", "dx", 16, lambda e: e.dma_start(self.c_bfc, din["bfc2d"]))
        P.inc("sync", "dx", 16, lambda e: e.dma_start(self.c_bpj, din["bpj2d"]))
        w_cd = v(O["WCD"], BF16, [NKC, 512])
        w_qup = v(O["WQUP"], BF16, [NLC, 1024])
        w_kv = v(O["WKV"], BF16, [NLC, 128])
        P.dma_inc("gpsimd", "dw", lambda e: e.dma_start(
            w_cd, din["wcd"].rearrange("(kc p) o -> p kc o", p=128)))
        P.dma_inc("gpsimd", "dw", lambda e: e.dma_start(
            w_qup, din["wqup"].rearrange("(kc p) o -> p kc o", p=128)))
        P.dma_inc("gpsimd", "dw", lambda e: e.dma_start(
            w_kv, din["wkv"].rearrange("(kc p) o -> p kc o", p=128)))
        P.inc("vector", "dve", 1, lambda e: e.memset(self.c_ones_col, 1.0))
        P.inc("vector", "dve", 1, lambda e: e.memset(self.c_ones_row, 1.0))
        P.inc("vector", "dve", 1, lambda e: e.memset(self.c_ones_row_f, 1.0))
        P.inc("vector", "dve", 1, lambda e: e.memset(self.c_eps, 1e-5))

        self.p1buf = [v(O["P1"] + b * 8 * KB, F32, [2048]) for b in range(2)]
        M = O["MISC"]
        self.wfc_acc = v(M + 4 * KB, F32, [1])
        self.wpj_acc = v(M + 4 * KB + 64, F32, [1])
        self.p1tmp = v(M + 4 * KB + 128, F32, [1])
        self.p1_tiles = []
        for kc in range(NKC):
            for half in range(2):
                self.p1_tiles.append(
                    (din["wfcT"][kc * 128:(kc + 1) * 128,
                                 half * 2048:(half + 1) * 2048], 0))
        for c4 in range(16):
            self.p1_tiles.append(
                (din["wpjT"][c4 * 256:(c4 + 1) * 256, :]
                 .rearrange("(a p) o -> p a o", p=128), 1))
        self.p1_dma_ck = {}
        self.p1_red_ck = {}
        self.p1_dma(0)
        self.p1_dma(1)
        p1n = 2

        xseg = [v(O["XSEG"] + b * 16 * KB, F32, [NKC, SEG]) for b in range(2)]
        xnb = [v(O["XNB"] + b * 8 * KB, BF16, [NKC, SEG]) for b in range(2)]

        seg_done_dve = {}
        seg_done_act = {}
        seg_done_pe = {}
        for seg in range(NSEG):
            b = seg % 2
            if seg >= 2:
                P.wait("sync", "dve", seg_done_dve[seg - 2])
                P.wait("sync", "act", seg_done_act[seg - 2])
            for kc in range(NKC):
                P.inc("sync", "dx", 16, lambda e, kc=kc, seg=seg, b=b: e.dma_start(
                    xseg[b][:, kc, :], din["xT"][kc * 128:(kc + 1) * 128,
                                                 seg * SEG:(seg + 1) * SEG]))

            def pre_cast_wait(seg=seg):
                P.wait("vector", "dx")
                P.wait("scalar", "dx")
                if seg >= 2:
                    P.wait("vector", "pe", seg_done_pe[seg - 2])

            src = [xseg[b][:, kc, :] for kc in range(NKC)]
            dst = [xnb[b][:, kc, :] for kc in range(NKC)]
            self.emit_layernorm_seg(src, dst, (0, 1, 2, 3), pre_cast_wait)
            seg_done_dve[seg] = P.ck("dve")
            seg_done_act[seg] = P.ck("act")
            P.wait("tensor", "act")
            P.wait("tensor", "dve")
            for kc in range(NKC):
                for oc in range(NLC):
                    P.inc("tensor", "pe", 1, P.mm(
                        self.pbank(4 + oc),
                        w_cd[:, kc, oc * 128:(oc + 1) * 128],
                        xnb[b][:, kc, :], start=(kc == 0), stop=(kc == NKC - 1)))
            seg_done_pe[seg] = P.ck("pe")
            P.wait("scalar", "pe")
            for oc in range(NLC):
                P.inc("scalar", "act", 1, lambda e, oc=oc, seg=seg: e.activation(
                    self.latT[:, oc, seg * SEG:(seg + 1) * SEG],
                    self.pbank(4 + oc), AF.Copy))
            for _ in range(8):
                self.p1_reduce(p1n - 2)
                self.p1_dma(p1n)
                p1n += 1

        P.wait("tensor", "act")
        for lc in range(NLC):
            for seg in range(NSEG):
                P.inc("tensor", "pe", 1, P.mm(
                    self.pbank(4 + seg), w_kv[:, lc, :],
                    self.latT[:, lc, seg * SEG:(seg + 1) * SEG],
                    start=(lc == 0), stop=(lc == NLC - 1)))
        P.wait("scalar", "pe")
        for seg in range(NSEG):
            P.inc("scalar", "act", 1, lambda e, seg=seg: e.activation(
                self.kvT[:, seg * SEG:(seg + 1) * SEG],
                self.pbank(4 + seg), AF.Copy))
        for oc in range(NKC):
            P.wait("tensor", "act")
            for lc in range(NLC):
                for seg in range(NSEG):
                    P.inc("tensor", "pe", 1, P.mm(
                        self.pbank(seg),
                        w_qup[:, lc, oc * 128:(oc + 1) * 128],
                        self.latT[:, lc, seg * SEG:(seg + 1) * SEG],
                        start=(lc == 0), stop=(lc == NLC - 1)))
            P.wait("scalar", "pe")
            for seg in range(NSEG):
                P.inc("scalar", "act", 1, lambda e, oc=oc, seg=seg: e.activation(
                    self.qT[:, oc, seg * SEG:(seg + 1) * SEG],
                    self.pbank(seg), AF.Copy, scale=0.125))
            if oc % 2 == 0:
                self.p1_reduce(p1n - 2)
                self.p1_dma(p1n)
                p1n += 1

        P.wait("vector", "act")
        P.inc("vector", "dve", 1, lambda e: e.tensor_copy(
            self.k2, self.kvT[0:64, :]))
        P.inc("vector", "dve", 1, lambda e: e.memset(
            v(O["VEXT"], BF16, [NT * 65]), 1.0))
        ps_tr = self.PB[:, 2048:2048 + 32].bitcast(BF16)
        P.wait("tensor", "act")
        for tt in range(NT):
            P.wait("tensor", "dve")
            P.inc("tensor", "pe", 1, lambda e, tt=tt: e.transpose(
                ps_tr, self.kvT[64:128, tt * 128:(tt + 1) * 128],
                self.c_ident[64:128, 64:128]))
            P.wait("vector", "pe")
            P.inc("vector", "dve", 1, lambda e, tt=tt: e.tensor_copy(
                self.vext[:, tt, 0:64], ps_tr))
            if tt % 2 == 0:
                self.p1_reduce(p1n - 2)
                self.p1_dma(p1n)
                p1n += 1
        while p1n - 2 < len(self.p1_tiles):
            self.p1_reduce(p1n - 2)
            self.p1_dma(p1n)
            p1n += 1

        qacc = v(M + 2 * KB, BF16, [1])
        tmp1 = v(M + 2 * KB + 64, BF16, [1])
        r128 = v(M + 2 * KB + 128, BF16, [128], nparts=1)
        scb = v(M + 2 * KB + 512, BF16, [8], nparts=1)
        scf = v(M + 3 * KB, F32, [8], nparts=1)
        P.wait("vector", "act")
        for oc in range(NKC):
            if oc == 0:
                P.inc("vector", "dve", 1, lambda e: e.reduce_max(
                    qacc, self.qT[:, 0, :], axis=AX.X, apply_absolute_value=True))
            else:
                P.inc("vector", "dve", 1, lambda e, oc=oc: e.reduce_max(
                    tmp1, self.qT[:, oc, :], axis=AX.X,
                    apply_absolute_value=True))
                P.wait("vector", "dve")
                P.inc("vector", "dve", 1, lambda e: e.tensor_max(
                    qacc, qacc, tmp1))
        ps_r = self.PB[0:1, 2048 + 256:2048 + 256 + 64].bitcast(BF16)
        P.wait("tensor", "dve")
        P.inc("tensor", "pe", 1, lambda e: e.transpose(ps_r, qacc, self.c_ident))
        P.wait("vector", "pe")
        P.inc("vector", "dve", 1, lambda e: e.tensor_copy(r128, ps_r))
        P.wait("vector", "dve")
        P.inc("vector", "dve", 1, lambda e: e.reduce_max(
            scb[:, 0:1], r128, axis=AX.X))
        P.inc("vector", "dve", 1, lambda e: e.reduce_max(
            tmp1[0:64, :], self.kvT[0:64, :], axis=AX.X,
            apply_absolute_value=True))
        P.wait("tensor", "dve")
        P.inc("tensor", "pe", 1, lambda e: e.transpose(
            ps_r[0:1, 0:64], tmp1[0:64, :], self.c_ident[0:64, 0:64]))
        P.wait("vector", "pe")
        P.inc("vector", "dve", 1, lambda e: e.tensor_copy(
            r128[0:1, 0:64], ps_r[0:1, 0:64]))
        P.wait("vector", "dve")
        P.inc("vector", "dve", 1, lambda e: e.reduce_max(
            scb[:, 1:2], r128[0:1, 0:64], axis=AX.X))
        P.wait("vector", "dve")
        P.inc("vector", "dve", 1, lambda e: e.tensor_mul(
            scf[:, 0:1], scb[:, 0:1], scb[:, 1:2]))
        P.wait("vector", "dve")
        P.inc("vector", "dve", 1, lambda e: e.tensor_scalar(
            scf[:, 1:2], scf[:, 0:1], -64.0, None, ALU.mult))
        P.wait("tensor", "dve")
        P.inc("tensor", "pe", 1, P.mm(self.PB[:, 2048 + 512:2048 + 513],
                                      self.c_ones_row_f, scf[:, 1:2]))
        P.wait("vector", "pe")
        P.inc("vector", "dve", 1, lambda e: e.tensor_copy(
            self.c_negc, self.PB[:, 2048 + 512:2048 + 513]))
        self.negc_ready = P.ck("dve")

        sfs = v(M + 5 * KB, F32, [16], nparts=1)
        P.drain_inc("vector", "dve")
        P.wait("sync", "dve")
        P.dma_inc("sync", "dwf", lambda e: e.dma_start(
            self.scr_acc[0, :], self.wfc_acc))
        P.dma_inc("sync", "dwf", lambda e: e.dma_start(
            self.scr_acc[1, :], self.wpj_acc))
        P.wait("sync", "dwf")
        accrow = v(M + 6 * KB, F32, [2, 128], nparts=1)
        P.dma_inc("sync", "dwf", lambda e: e.dma_start(
            accrow, self.scr_acc.rearrange("a p -> (a p)")))
        acc_row_ck = P.ck("dwf")
        for wi, (rsc, ssc, nsc) in enumerate([
                (self.c_rscale_fc, self.c_scale_fc, self.c_nsc_fc),
                (self.c_rscale_pj, self.c_scale_pj, self.c_nsc_pj)]):
            P.wait("vector", "dwf", acc_row_ck)
            P.inc("vector", "dve", 1, lambda e, wi=wi: e.reduce_max(
                sfs[:, wi * 4:wi * 4 + 1], accrow[:, wi, :], axis=AX.X))
            P.wait("vector", "dve")
            P.inc("vector", "dve", 1, lambda e, wi=wi: e.reciprocal(
                sfs[:, wi * 4 + 1:wi * 4 + 2], sfs[:, wi * 4:wi * 4 + 1]))
            P.wait("vector", "dve")
            P.inc("vector", "dve", 1, lambda e, wi=wi: e.tensor_scalar(
                sfs[:, wi * 4 + 2:wi * 4 + 3], sfs[:, wi * 4 + 1:wi * 4 + 2],
                127.0, None, ALU.mult))
            P.inc("vector", "dve", 1, lambda e, wi=wi: e.tensor_scalar(
                sfs[:, wi * 4 + 3:wi * 4 + 4], sfs[:, wi * 4:wi * 4 + 1],
                -1.0 / 127.0, None, ALU.mult))
            for si, dstv in ((2, rsc), (3, nsc)):
                P.wait("tensor", "dve")
                P.inc("tensor", "pe", 1, P.mm(
                    self.PB[:, 128:129], self.c_ones_row_f,
                    sfs[:, wi * 4 + si:wi * 4 + si + 1]))
                P.wait("vector", "pe")
                P.inc("vector", "dve", 1, lambda e, dstv=dstv: e.tensor_copy(
                    dstv, self.PB[:, 128:129]))
            P.wait("vector", "dve")
            P.inc("vector", "dve", 1, lambda e, ssc=ssc, nsc=nsc: e.tensor_scalar(
                ssc, nsc, -1.0, None, ALU.mult))
        self.scales_ready = P.ck("dve")

        LO = O["LORA"]
        self.afcTb = v(LO, BF16, [NKC, 16])
        self.apjTb = v(LO + 256, BF16, [NFC, 16])
        self.bfcTb = v(LO + 1280, BF16, [FF], nparts=16)
        self.bpjTb = v(LO + 9472, BF16, [D], nparts=16)
        self.rt1b = v(self.O["RT1B"], BF16, [SEG], nparts=16)
        self.rt2b = v(self.O["RT2B"], BF16, [SEG], nparts=16)
        P.wait("gpsimd", "dve", self.scales_ready)
        P.dma_inc("gpsimd", "dw", lambda e: e.dma_start(
            self.afcTb, self.din["afcT"].rearrange("(kc p) r -> p kc r", p=128)))
        P.dma_inc("gpsimd", "dw", lambda e: e.dma_start(
            self.apjTb, self.din["apjT"].rearrange("(kc p) r -> p kc r", p=128)))
        P.dma_inc("gpsimd", "dw", lambda e: e.dma_start(
            self.bfcTb, self.din["bfcT"]))
        P.dma_inc("gpsimd", "dw", lambda e: e.dma_start(
            self.bpjTb, self.din["bpjT"]))
        self.lora_ready = P.ck("dw")

    def p1_dma(self, i):
        P = self
        if i >= len(self.p1_tiles):
            return
        if i >= 2:
            P.wait("sync", "dve", self.p1_red_ck[i - 2])
        src, _ = self.p1_tiles[i]
        dst = self.p1buf[i % 2]
        if len(src.shape) == 3:
            dst = dst.rearrange("p (a o) -> p a o", a=src.shape[1])
        P.dma_inc("sync", "dwf", lambda e, i=i, src=src, dst=dst: e.dma_start(
            dst, src))
        self.p1_dma_ck[i] = P.ck("dwf")

    def p1_reduce(self, i):
        P = self
        if i < 0 or i >= len(self.p1_tiles) or i in self.p1_red_ck:
            return
        _, acc = self.p1_tiles[i]
        accv = self.wfc_acc if acc == 0 else self.wpj_acc
        first = (i == 0) or (acc == 1 and self.p1_tiles[i - 1][1] == 0)
        P.wait("vector", "dwf", self.p1_dma_ck[i])
        if first:
            P.inc("vector", "dve", 1, lambda e, i=i, accv=accv: e.reduce_max(
                accv, self.p1buf[i % 2], axis=AX.X, apply_absolute_value=True))
        else:
            P.inc("vector", "dve", 1, lambda e, i=i: e.reduce_max(
                self.p1tmp, self.p1buf[i % 2], axis=AX.X,
                apply_absolute_value=True))
            P.wait("vector", "dve")
            P.inc("vector", "dve", 1, lambda e, accv=accv: e.tensor_max(
                accv, accv, self.p1tmp))
        self.p1_red_ck[i] = P.ck("dve")

    def setup_p2(self):
        P, v, O = self, self.view, self.O
        PP = O["P2Q"]
        self.p2w = [v(PP + b * 2 * KB, F32, [SEG]) for b in range(2)]
        self.p2i = v(PP + 4 * KB, I32, [SEG])
        self.p2u = v(PP + 4 * KB, F32, [SEG])
        self.p2f = v(PP + 6 * KB, F32, [SEG])
        self.p2r = v(PP + 8 * KB, F32, [SEG])
        self.p2s = [v(PP + 10 * KB + b * KB, BF16, [SEG]) for b in range(2)]
        self.p2chk = v(self.O["CONST"] + 2304, BF16, [8])
        self.p2_tiles = []
        for cb in range(8):
            for kc in range(NKC):
                self.p2_tiles.append(("fc", kc, cb))
        for cb2 in range(2):
            for kc2 in range(NFC):
                self.p2_tiles.append(("pj", kc2, cb2))
        self.p2_next = 0
        self.p2_dma_ck = {}
        self.p2_done = {}
        self.p2_scr_ck = {}

    def p2_tile(self):
        P, din = self, self.din
        i = self.p2_next
        if i >= len(self.p2_tiles):
            return False
        self.p2_next += 1
        kind, kc, cb = self.p2_tiles[i]
        src = (din["wfcT"] if kind == "fc" else din["wpjT"])[
            kc * 128:(kc + 1) * 128, cb * 512:(cb + 1) * 512]
        dstd = (self.scr_fc if kind == "fc" else self.scr_pj)[cb, :, kc, :]
        if i >= 2:
            P.wait("sync", "dve", self.p2_done[i - 2])
        P.dma_inc("sync", "dwf", lambda e, i=i, src=src: e.dma_start(
            self.p2w[i % 2], src))
        self.p2_dma_ck[i] = P.ck("dwf")
        rsc = self.c_rscale_fc if kind == "fc" else self.c_rscale_pj
        nsc = self.c_nsc_fc if kind == "fc" else self.c_nsc_pj
        P.wait("vector", "dwf", self.p2_dma_ck[i])
        if i == 0:
            P.wait("vector", "dve", self.scales_ready)
        P.inc("vector", "dve", 1, lambda e, i=i, rsc=rsc: e.tensor_scalar(
            self.p2r, self.p2w[i % 2], rsc[:, 0:1], 256.5, ALU.mult, ALU.add))
        P.wait("vector", "dve")
        P.inc("vector", "dve", 1, lambda e: e.tensor_copy(self.p2i, self.p2r))
        P.wait("vector", "dve")
        P.inc("vector", "dve", 1, lambda e: e.tensor_copy(self.p2f, self.p2i))
        P.wait("vector", "dve")
        P.inc("vector", "dve", 1, lambda e: e.tensor_sub(
            self.p2r, self.p2r, self.p2f))
        P.wait("vector", "dve")
        P.inc("vector", "dve", 1, lambda e: e.scalar_tensor_tensor(
            self.p2u, self.p2r, 0.0, self.p2f, ALU.is_lt, ALU.subtract))
        P.wait("vector", "dve")
        if i >= 2:
            P.wait("vector", "dscr", 16 * (i - 1))
        P.inc("vector", "dve", 1, lambda e, i=i, nsc=nsc: e.tensor_scalar(
            self.p2s[i % 2], self.p2u, 256.0, nsc[:, 0:1], ALU.add, ALU.mult))
        P.drain_inc("vector", "dve")
        self.p2_done[i] = P.ck("dve")
        P.wait("gpsimd", "dve", self.p2_done[i])
        P.dma_inc("gpsimd", "dscr", lambda e, i=i, dstd=dstd: e.dma_start(
            dstd, self.p2s[i % 2]))
        self.p2_scr_ck[i] = P.ck("dscr")
        return True

    def p2_fc_ready(self, cb):
        while cb * 8 + 7 not in self.p2_scr_ck:
            self.p2_tile()
        return self.p2_scr_ck[cb * 8 + 7]

    def p2_pj_ready(self, cb2, kh4):
        while 64 + cb2 * 32 + kh4 * 8 + 7 not in self.p2_scr_ck:
            self.p2_tile()
        return self.p2_scr_ck[64 + cb2 * 32 + kh4 * 8 + 7]

    def enqueue_ffn(self, g):
        P = self
        xn2 = self.xn2g[g % 2]
        x1 = self.x1g[g % 2]
        hT = self.hT
        gcols = slice(g * SEG, (g + 1) * SEG)
        xn2_ck = self.xn2_ck[g]
        x1_ck = self.x1_ck[g]
        bgq = self.bgq
        rt_ps = self.pbank(7)
        wfc_t = self.wfc_t
        wpj_t = self.wpj_t

        def c_rt1():
            P.wait("tensor", "dw", self.lora_ready)
            P.wait("tensor", "dve", xn2_ck)
            P.wait("tensor", "dve", self.fin_ck.get((g - 1, 1, 1), 0))
            P.wait("tensor", "act", self.rt2_evac_ck.get(g - 1, 0))
            for kc in range(NKC):
                P.inc("tensor", "pe", 1, P.mm(
                    rt_ps[0:16, :], self.afcTb[:, kc, :], xn2[:, kc, :],
                    start=(kc == 0), stop=(kc == NKC - 1)))
            ck = P.ck("pe")
            P.wait("scalar", "pe", ck)
            P.inc("scalar", "act", 1, lambda e: e.activation(
                self.rt1b, rt_ps[0:16, :], AF.Copy, scale=2.0))
            self.rt1_evac_ck[g] = P.ck("act")
        bgq.append(c_rt1)
        self.n_queued += 1

        for ocg in range(8):
            def c_wfc(ocg=ocg):
                i = self.wfc_i
                self.wfc_i += 1
                P.wait("gpsimd", "dscr", self.p2_fc_ready(ocg))
                if i >= 2:
                    P.wait("gpsimd", "pe", self.wfc_used_ck[i - 2])
                P.dma_inc("gpsimd", "dffn", lambda e, ocg=ocg, i=i: e.dma_start(
                    wfc_t[i % 2], self.scr_fc[ocg]))
                self.wfc_dma_ck[i] = P.ck("dffn")
            bgq.append(c_wfc)
            self.n_queued += 1
            for jj in range(4):
                def c_fc(ocg=ocg, jj=jj):
                    i = self.wfc_i - 1
                    oc = ocg * 4 + jj
                    pb = self.pbank(6 + oc % 2)
                    P.wait("tensor", "dffn", self.wfc_dma_ck[i])
                    if oc >= 2:
                        P.wait("tensor", "act", self.gelu_ck[(g, oc - 2)])
                    elif oc == 1:
                        P.wait("tensor", "act", self.rt1_evac_ck[g])
                    else:
                        P.wait("tensor", "act",
                               self.gelu_ck.get((g - 1, 30), 0))
                    for kc in range(NKC):
                        P.inc("tensor", "pe", 1, P.mm(
                            pb, wfc_t[i % 2][:, kc, jj * 128:(jj + 1) * 128],
                            xn2[:, kc, :], start=(kc == 0), stop=False))
                    P.inc("tensor", "pe", 1, P.mm(
                        pb, self.bfcTb[:, oc * 128:(oc + 1) * 128],
                        self.rt1b, start=False, stop=True))
                    mmck = P.ck("pe")
                    if jj == 3:
                        self.wfc_used_ck[i] = mmck
                    P.wait("scalar", "pe", mmck)
                    P.inc("scalar", "act", 1,
                          lambda e, oc=oc, pb=pb: e.activation(
                              hT[:, oc, :], pb, AF.Gelu,
                              bias=self.c_bfc[:, oc:oc + 1]))
                    self.gelu_ck[(g, oc)] = P.ck("act")
                bgq.append(c_fc)
                self.n_queued += 1
        self.fc_end_q[g] = self.n_queued

        def c_rt2():
            P.wait("tensor", "act", self.gelu_ck[(g, 31)])
            for kc2 in range(NFC):
                P.inc("tensor", "pe", 1, P.mm(
                    rt_ps[0:16, :], self.apjTb[:, kc2, :], hT[:, kc2, :],
                    start=(kc2 == 0), stop=(kc2 == NFC - 1)))
            ck = P.ck("pe")
            P.wait("scalar", "pe", ck)
            P.inc("scalar", "act", 1, lambda e: e.activation(
                self.rt2b, rt_ps[0:16, :], AF.Copy, scale=2.0))
            self.rt2_evac_ck[g] = P.ck("act")
        bgq.append(c_rt2)
        self.n_queued += 1

        for cb2 in range(2):
            for jp in range(2):
                for kh4 in range(4):
                    def c_wpj(cb2=cb2, jp=jp, kh4=kh4):
                        P.wait("gpsimd", "dscr", self.p2_pj_ready(cb2, kh4))
                        P.wait("gpsimd", "pe")
                        P.dma_inc("gpsimd", "dffn",
                              lambda e, cb2=cb2, kh4=kh4: e.dma_start(
                                  wpj_t, self.scr_pj[cb2, :,
                                                     kh4 * 8:(kh4 + 1) * 8, :]))
                    bgq.append(c_wpj)
                    self.n_queued += 1

                    def c_pj(cb2=cb2, jp=jp, kh4=kh4):
                        P.wait("tensor", "dffn")
                        if kh4 == 0:
                            P.wait("tensor", "act", self.rt2_evac_ck[g])
                            prev = (g, cb2, jp - 1) if jp else (g, cb2 - 1, 1)
                            P.wait("tensor", "dve",
                                   self.fin_ck.get(prev, 0))
                        for k8 in range(8):
                            kc2 = kh4 * 8 + k8
                            for jh in range(2):
                                pb = self.pbank(6 + jh)
                                P.inc("tensor", "pe", 1, P.mm(
                                    pb, wpj_t[:, k8,
                                              (jp * 2 + jh) * 128:
                                              (jp * 2 + jh) * 128 + 128],
                                    hT[:, kc2, :],
                                    start=(kc2 == 0), stop=False))
                    bgq.append(c_pj)
                    self.n_queued += 1

                def c_pj_fin(cb2=cb2, jp=jp, gcols=gcols, x1=x1, x1_ck=x1_ck):
                    for jh in range(2):
                        oc2 = cb2 * 4 + jp * 2 + jh
                        pb = self.pbank(6 + jh)
                        P.inc("tensor", "pe", 1, P.mm(
                            pb, self.bpjTb[:, oc2 * 128:(oc2 + 1) * 128],
                            self.rt2b, start=False, stop=True))
                    mmck = P.ck("pe")
                    P.wait("vector", "pe", mmck)
                    P.wait("vector", "dve", x1_ck)
                    for jh in range(2):
                        oc2 = cb2 * 4 + jp * 2 + jh
                        pb = self.pbank(6 + jh)
                        P.inc("vector", "dve", 1,
                              lambda e, oc2=oc2, pb=pb: e.scalar_tensor_tensor(
                                  x1[:, oc2, :], pb,
                                  self.c_bpj[:, oc2:oc2 + 1],
                                  x1[:, oc2, :], ALU.add, ALU.add))
                    P.drain_inc("vector", "dve")
                    dck = P.ck("dve")
                    P.wait("sync", "dve", dck)
                    for jh in range(2):
                        oc2 = cb2 * 4 + jp * 2 + jh
                        P.dma_inc("sync", "dout",
                              lambda e, oc2=oc2, gcols=gcols: e.dma_start(
                                  self.outT[oc2 * 128:(oc2 + 1) * 128, gcols],
                                  x1[:, oc2, :]))
                    self.fin_ck[(g, cb2, jp)] = P.ck("dve")
                bgq.append(c_pj_fin)
                self.n_queued += 1
        self.ffn_end_q[g] = self.n_queued

    def pump(self, n):
        for _ in range(n):
            if not self.bgq:
                return
            self.n_pumped += 1
            self.bgq.popleft()()

    def pump_until(self, target):
        while self.n_pumped < target and self.bgq:
            self.pump(1)

    def phase_main(self):
        P, v, O = self, self.view, self.O
        self.bgq = collections.deque()
        self.n_pumped = 0
        self.n_queued = 0
        self.fc_end_q = {}
        self.setup_p2()
        self.wfc_t = [v(O["WFC_D"] + b * 8 * KB, BF16, [NKC, 512])
                      for b in range(2)]
        self.wpj_t = v(O["WPJ_D"], BF16, [8, 512])
        self.wfc_i = 0
        self.wfc_dma_ck = {}
        self.wfc_used_ck = {}
        self.hT = v(O["HT"], BF16, [NFC, SEG])
        self.x1g = [v(O["X1G"] + b * 16 * KB, F32, [NKC, SEG]) for b in range(2)]
        self.xn2g = [v(O["XN2G"], BF16, [NKC, SEG])] * 2
        self.ytg = v(O["YTG"], BF16, [NKC, SEG])
        self.xn2_ck = {}
        self.x1_ck = {}
        self.gelu_ck = {}
        self.rt1_evac_ck = {}
        self.rt2_evac_ck = {}
        self.fin_ck = {}
        self.ffn_end_q = {}
        self.wcp_used_ck = {}
        wcp = [v(O["WCP_S"] + b * 2 * KB, BF16, [NKC, 128]) for b in range(2)]
        xres = [v(O["XRES"] + b * 2 * KB, F32, [SEG]) for b in range(2)]
        rnB = v(O["RNB"], F32, [SEG], nparts=64)
        rn_row = v(O["STATS"], F32, [1024], nparts=1)
        rn_bf = v(O["STATS"] + 4 * KB, BF16, [T], nparts=1)
        attT = [v(O["ATT"] + b * 2 * KB, BF16, [1024]) for b in range(2)]
        sc_ps = self.PB[:, 0:1024]
        yext = self.PB[:, 1024:3072]

        def q_mov(qt, par, cg):
            base = 64 * par
            return self.qT[base:base + 64, cg * 4:(cg + 1) * 4,
                           qt * 128:(qt + 1) * 128]

        exp_ck = {}
        av_ck = {}
        mask_ck = {}
        yev_ck = 0
        j = 0
        pend = []

        def emit_av(item):
            jj, qt, sc, hg, buf = item
            P.wait("tensor", "act", exp_ck[jj])
            if sc == qt:
                P.wait("tensor", "pool", mask_ck[jj])
            P.inc("tensor", "pe", 1, P.mm(
                yext[0:65, hg * 1024: hg * 1024 + 512],
                self.vext[:, sc, :], attT[buf][:, 0:512],
                start=(sc == 0), stop=(sc == qt)))
            av_ck[jj] = P.inc("tensor", "pe", 1, P.mm(
                yext[0:65, hg * 1024 + 512: hg * 1024 + 1024],
                self.vext[:, sc, :], attT[buf][:, 512:1024],
                start=(sc == 0), stop=(sc == qt)))

        pace = [2, 4, 6, 8, 10, 12, 12, 12, 12, 12, 12, 12, 10, 6, 0, 0]
        for g in range(4):
            grp_start_dve = P.ck("dve")
            for qt in range(4 * g, 4 * g + 4):
                for _ in range(pace[qt]):
                    if not self.p2_tile():
                        break
                for sc in range(qt + 1):
                    for hg in range(2):
                        buf = j % 2
                        if j >= 1:
                            P.wait("tensor", "act", exp_ck[j - 1])
                        if sc == 0 and hg == 0:
                            if qt % 4 == 0:
                                P.wait("tensor", "dve", grp_start_dve)
                            if qt > 0:
                                P.wait("tensor", "dve", yev_ck)
                        lhsK = self.kvT[0:64, sc * 128:(sc + 1) * 128]
                        P.inc("tensor", "pe", 1, P.mm(
                            sc_ps[:, 0:512], lhsK, q_mov(qt, 0, hg),
                            tp=(0, 0)))
                        qk_ck = P.inc("tensor", "pe", 1, P.mm(
                            sc_ps[:, 512:1024],
                            self.k2[:, sc * 128:(sc + 1) * 128],
                            q_mov(qt, 1, hg), tp=(64, 0)))
                        if pend:
                            emit_av(pend.pop(0))
                        P.wait("scalar", "pe", qk_ck)
                        if j >= 2:
                            P.wait("scalar", "pe", av_ck.get(j - 2, 0))
                        if j == 0:
                            P.wait("scalar", "dve", self.negc_ready)
                        P.inc("scalar", "act", 1, lambda e, buf=buf: e.activation(
                            attT[buf], sc_ps, AF.Exp, bias=self.c_negc[:, 0:1]))
                        exp_ck[j] = P.ck("act")
                        if sc == qt:
                            P.wait("gpsimd", "act", exp_ck[j])
                            for jj in range(8):
                                P.inc("gpsimd", "pool", 1,
                                      lambda e, buf=buf, jj=jj: e.tensor_mul(
                                          attT[buf][:, jj * 128:(jj + 1) * 128],
                                          attT[buf][:, jj * 128:(jj + 1) * 128],
                                          self.c_tril))
                            mask_ck[j] = P.ck("pool")
                        pend.append((j, qt, sc, hg, buf))
                        self.pump(1)
                        j += 1
                while pend:
                    emit_av(pend.pop(0))
                P.wait("scalar", "pe", av_ck[j - 1])
                for hh in range(2):
                    P.inc("scalar", "act", 1, lambda e, hh=hh: e.activation(
                        rn_row, yext[64:65, hh * 1024:(hh + 1) * 1024], AF.Ln))
                    P.wait("scalar", "act")
                    P.inc("scalar", "act", 1, lambda e, hh=hh: e.activation(
                        rn_bf[0:1, hh * 1024:(hh + 1) * 1024], rn_row,
                        AF.Exp, scale=-1.0))
                P.wait("tensor", "act")
                for half in range(2):
                    for q2 in range(2):
                        c0 = half * 1024 + q2 * 512
                        P.inc("tensor", "pe", 1, P.mm(
                            sc_ps[0:64, q2 * 512:q2 * 512 + 512],
                            self.c_ones_row[0:1, 0:64],
                            rn_bf[0:1, c0:c0 + 512]))
                    P.wait("vector", "pe")
                    for par in range(2):
                        cols0 = half * 1024 + par * 512
                        dst = self.ytg[par * 64:par * 64 + 64,
                                       half * 4:(half + 1) * 4,
                                       (qt % 4) * 128:(qt % 4) * 128 + 128]
                        P.inc("vector", "dve", 1,
                              lambda e, par=par: e.tensor_copy(
                                  rnB, sc_ps[0:64, par * 512:par * 512 + 512]))
                        P.wait("vector", "dve")
                        P.inc("vector", "dve", 1,
                              lambda e, dst=dst, cols0=cols0: e.tensor_mul(
                                  dst, yext[0:64, cols0:cols0 + 512], rnB))
                    if half == 0:
                        P.wait("tensor", "dve")
                yev_ck = P.ck("dve")
                self.pump(3)
            self.group_tail(g, wcp, xres)
        while self.bgq:
            self.bgq.popleft()()

        if self.debug:
            P.wait("sync", "act")
            P.wait("sync", "dve")
            dbg = self.dbg
            P.dma_inc("sync", "dout", lambda e: e.dma_start(
                dbg["d_qT"], v(O["QT"], BF16, [NKC * T])))
            P.dma_inc("sync", "dout", lambda e: e.dma_start(
                dbg["d_kvT"], self.kvT))
            P.dma_inc("sync", "dout", lambda e: e.dma_start(
                dbg["d_negc"], self.c_negc))
            P.wait("sync", "dscr")
            for cb in range(8):
                P.dma_inc("sync", "dout", lambda e, cb=cb: e.dma_start(
                    dbg["d_wfc"][cb], self.scr_fc[cb]))
            for cb in range(2):
                P.dma_inc("sync", "dout", lambda e, cb=cb: e.dma_start(
                    dbg["d_wpj"][cb], self.scr_pj[cb]))
            P.wait("sync", "pe")
            P.wait("sync", "pool")
            P.dma_inc("sync", "dout", lambda e: e.dma_start(
                dbg["d_ytg"], v(O["YTG"], BF16, [NKC * SEG])))
            P.dma_inc("sync", "dout", lambda e: e.dma_start(
                dbg["d_x1g0"], v(O["X1G"], F32, [NKC * SEG])))
            P.dma_inc("sync", "dout", lambda e: e.dma_start(
                dbg["d_x1g1"], v(O["X1G"] + 16 * KB, F32, [NKC * SEG])))
            P.dma_inc("sync", "dout", lambda e: e.dma_start(
                dbg["d_xn2"], v(O["XN2G"], BF16, [NKC * SEG])))
            P.dma_inc("sync", "dout", lambda e: e.dma_start(
                dbg["d_hT"], v(O["HT"], BF16, [NFC * SEG])))
            P.dma_inc("sync", "dout", lambda e: e.dma_start(
                dbg["d_rt"], v(O["RT1B"], BF16, [2 * SEG], nparts=16)))
            P.dma_inc("sync", "dout", lambda e: e.dma_start(
                dbg["d_rnbf"], v(O["STATS"] + 4 * KB, BF16, [T], nparts=1)))
            P.dma_inc("sync", "dout", lambda e: e.dma_start(
                dbg["d_const"], v(O["CONST"], F32, [1024])))
            P.dma_inc("sync", "dout", lambda e: e.dma_start(
                dbg["d_acc"], v(O["MISC"] + 6 * KB, F32, [256], nparts=1)))

    def group_tail(self, g, wcp, xres):
        P, v, O, din = self, self.view, self.O, self.din
        gcols = slice(g * SEG, (g + 1) * SEG)
        x1 = self.x1g[g % 2]
        xn2 = self.xn2g[g % 2]
        P.wait("tensor", "dve")
        if g >= 2:
            self.pump_until(self.ffn_end_q[g - 2])
        prev_add_ck = {}
        for oc in range(NKC):
            if oc >= 2:
                P.wait("gpsimd", "pe", self.wcp_used_ck[(g, oc - 2)])
            P.dma_inc("gpsimd", "dw", lambda e, oc=oc: e.dma_start(
                wcp[oc % 2],
                din["wcproj"].rearrange("(kc p) o -> p kc o", p=128)
                [:, :, oc * 128:(oc + 1) * 128]))
            wcp_ck = P.ck("dw")
            if oc >= 2:
                P.wait("sync", "dve", prev_add_ck[oc - 2])
            P.dma_inc("sync", "dres", lambda e, oc=oc, gcols=gcols: e.dma_start(
                xres[oc % 2], din["xT"][oc * 128:(oc + 1) * 128, gcols]))
            xr_ck = P.ck("dres")
            pb = self.pbank(oc % 2)
            P.wait("tensor", "dw", wcp_ck)
            if oc >= 2:
                P.wait("tensor", "dve", prev_add_ck[oc - 2])
            for kc in range(NKC):
                P.inc("tensor", "pe", 1, P.mm(
                    pb, wcp[oc % 2][:, kc, :], self.ytg[:, kc, :],
                    start=(kc == 0), stop=(kc == NKC - 1)))
            self.wcp_used_ck[(g, oc)] = P.ck("pe")
            P.wait("vector", "pe")
            P.wait("vector", "dres", xr_ck)
            P.wait("vector", "dout")
            P.inc("vector", "dve", 1, lambda e, oc=oc, x1=x1, pb=pb: e.tensor_add(
                x1[:, oc, :], pb, xres[oc % 2]))
            prev_add_ck[oc] = P.ck("dve")
            self.pump(1)
        self.x1_ck[g] = P.ck("dve")
        src = [x1[:, kc, :] for kc in range(NKC)]
        dst = [xn2[:, kc, :] for kc in range(NKC)]

        if g >= 1:
            self.pump_until(self.fc_end_q[g - 1])

        def ln2_wait():
            P.wait("scalar", "dve", self.x1_ck[g])
            if g >= 1:
                P.wait("vector", "pe")
                P.wait("scalar", "pe")

        self.emit_layernorm_seg(src, dst, (0, 1, 0, 1), ln2_wait)
        self.xn2_ck[g] = P.ck("dve")
        self.enqueue_ffn(g)


def build_program(debug=False):
    b = Builder(debug=debug)
    nc = b.build()
    return b, nc


def kernel(L0_attn_x, L0_attn_Wc_down, L0_attn_Wq_up, L0_attn_Wkv_up,
           L0_attn_Wc_proj, L0_ffn_W_fc, L0_ffn_b_fc, L0_ffn_A_fc,
           L0_ffn_B_fc, L0_ffn_W_proj, L0_ffn_b_proj, L0_ffn_A_proj,
           L0_ffn_B_proj):
    b, nc = build_program(debug=False)

    bf = ml_dtypes.bfloat16
    shared = {
        "wcd": np.ascontiguousarray(L0_attn_Wc_down, dtype=np.float32),
        "wqup": np.ascontiguousarray(L0_attn_Wq_up, dtype=np.float32),
        "wkv": np.ascontiguousarray(L0_attn_Wkv_up, dtype=np.float32),
        "wcproj": np.ascontiguousarray(L0_attn_Wc_proj, dtype=np.float32),
        "wfcT": np.ascontiguousarray(np.asarray(L0_ffn_W_fc).T, dtype=np.float32),
        "wpjT": np.ascontiguousarray(np.asarray(L0_ffn_W_proj).T, dtype=np.float32),
        "afcT": np.ascontiguousarray(np.asarray(L0_ffn_A_fc).T, dtype=np.float32),
        "bfcT": np.ascontiguousarray(np.asarray(L0_ffn_B_fc).T, dtype=np.float32),
        "apjT": np.ascontiguousarray(np.asarray(L0_ffn_A_proj).T, dtype=np.float32),
        "bpjT": np.ascontiguousarray(np.asarray(L0_ffn_B_proj).T, dtype=np.float32),
        "bfc2d": np.ascontiguousarray(
            np.asarray(L0_ffn_b_fc).reshape(NFC, 128).T, dtype=np.float32),
        "bpj2d": np.ascontiguousarray(
            np.asarray(L0_ffn_b_proj).reshape(NKC, 128).T, dtype=np.float32),
        "ident": np.eye(128, dtype=bf),
        "tril": (np.tril(np.ones((128, 128), dtype=np.float32)).T).astype(bf),
    }
    x = np.asarray(L0_attn_x)
    in_maps = []
    for c in range(8):
        m = dict(shared)
        m["xT"] = np.ascontiguousarray(x[c].T, dtype=np.float32)
        in_maps.append(m)
    res = run_bass_kernel_spmd(nc, in_maps, core_ids=list(range(8)))
    outs = []
    for c in range(8):
        o = np.asarray(res.results[c]["outT"])
        outs.append(np.ascontiguousarray(o.T))
    return np.stack(outs, axis=0)


# version 09
# speedup vs baseline: 1.0722x; 1.0722x over previous
import collections
import numpy as np
import ml_dtypes
from contextlib import ExitStack

import concourse.bass as bass
import concourse.mybir as mybir
from concourse.bass_utils import run_bass_kernel_spmd

F32 = mybir.dt.float32
BF16 = mybir.dt.bfloat16
I32 = mybir.dt.int32
AF = mybir.ActivationFunctionType
ALU = mybir.AluOpType
AX = mybir.AxisListType

T = 2048
D = 1024
NKC = D // 128
LAT = 512
NLC = LAT // 128
H = 16
HD = 64
FF = 4096
NFC = FF // 128
SEG = 512
NSEG = T // SEG
NT = T // 128
KB = 1024
HK = KB // 2

ENGS = ("sync", "tensor", "vector", "scalar", "gpsimd")


class Builder:
    def __init__(self, debug=False):
        self.debug = debug
        self.nc = bass.Bass("TRN2", target_bir_lowering=False,
                            detect_race_conditions=False)
        self.q = {e: [] for e in ENGS}
        self.cnt = {}
        self.handles = {}
        for s in ("dx", "dw", "dwf", "dscr", "dres", "dffn", "dout",
                  "pe", "act", "dve", "pool"):
            self.cnt[s] = 0

    def inc(self, eng, sem, n, build):
        self.cnt[sem] += n

        def f(e, build=build, sem=sem, n=n):
            build(e).then_inc(self.handles[sem], n)

        self.q[eng].append(f)
        return self.cnt[sem]

    def dma_inc(self, eng, sem, build):
        self.wait(eng, sem)
        return self.inc(eng, sem, 16, build)

    def drain_inc(self, eng, sem):
        self.cnt[sem] += 1

        def f(e, sem=sem):
            e.maybe_drain_then_inc((self.handles[sem], 1), fusable=False)

        self.q[eng].append(f)
        return self.cnt[sem]

    def wait(self, eng, sem, val=None):
        v = self.cnt[sem] if val is None else val
        if v <= 0:
            return
        assert v <= self.cnt[sem], \
            f"wait on future {sem} value {v} > planned {self.cnt[sem]}"

        def f(e, sem=sem, v=v):
            e.wait_ge(self.handles[sem], v)

        self.q[eng].append(f)

    def ck(self, sem):
        return self.cnt[sem]

    def mm(self, out, lhsT, rhs, start=True, stop=True, tp=None):
        def f(e, out=out, lhsT=lhsT, rhs=rhs, start=start, stop=stop, tp=tp):
            return e.matmul(out, lhsT, rhs, start=start, stop=stop,
                            tile_position=tp, skip_group_check=True)
        return f

    def alloc(self):
        nc = self.nc
        self.din = {}
        for name, shape, dt in [
            ("xT", [D, T], F32),
            ("wcd", [D, LAT], F32),
            ("wqup", [LAT, D], F32),
            ("wkv", [LAT, 2 * HD], F32),
            ("wcproj", [D, D], F32),
            ("wfcT", [D, FF], F32),
            ("wpjT", [FF, D], F32),
            ("afcT", [D, 16], F32),
            ("bfcT", [16, FF], F32),
            ("apjT", [FF, 16], F32),
            ("bpjT", [16, D], F32),
            ("bfc2d", [128, NFC], F32),
            ("bpj2d", [128, NKC], F32),
            ("ident", [128, 128], BF16),
            ("tril", [128, 128], BF16),
        ]:
            self.din[name] = nc.dram_tensor(name, shape, dt,
                                            kind="ExternalInput").ap()
        self.outT = nc.dram_tensor("outT", [D, T], F32,
                                   kind="ExternalOutput").ap()
        self.scr_fc = nc.dram_tensor("scr_fc", [8, 128, 8, 512], BF16,
                                     kind="Internal").ap()
        self.scr_pj = nc.dram_tensor("scr_pj", [2, 128, 32, 512], BF16,
                                     kind="Internal").ap()
        self.scr_acc = nc.dram_tensor("scr_acc", [2, 128], F32,
                                      kind="Internal").ap()
        self.dbg = {}
        if self.debug:
            for name, shape, dt in [
                ("d_qT", [128, NKC * T], BF16),
                ("d_kvT", [128, T], BF16),
                ("d_negc", [128, 1], F32),
                ("d_wfc", [8, 128, 8, 512], BF16),
                ("d_wpj", [2, 128, 32, 512], BF16),
                ("d_ytg", [128, NKC * SEG], BF16),
                ("d_x1g0", [128, NKC * SEG], F32),
                ("d_x1g1", [128, NKC * SEG], F32),
                ("d_xn2", [128, NKC * SEG], BF16),
                ("d_hT", [128, NFC * SEG], BF16),
                ("d_rt", [16, 2 * SEG], BF16),
                ("d_rnbf", [1, T], BF16),
                ("d_const", [128, 1024], F32),
                ("d_acc", [1, 256], F32),
            ]:
                self.dbg[name] = nc.dram_tensor(name, shape, dt,
                                                kind="ExternalOutput").ap()

        self.arena = nc.alloc_sbuf_tensor("arena", [128, 206 * KB],
                                          mybir.dt.uint8)[:, :]
        self.PB = nc.alloc_psum_tensor("pb", [128, 4096], F32)[:, :]

        O = {}
        off = 0

        def put(name, size):
            nonlocal off
            O[name] = off
            off += size
        put("CONST", 4096)
        put("QT", 32768)
        put("KVT", 4096)
        put("K2", 4096)
        put("VEXT", 2176)
        put("ATT", 4096)
        put("YTG", 8192)
        put("X1G", 32768)
        put("XN2G", 8192)
        put("HT", 32768)
        put("WFC_D", 16384)
        put("WPJ_D", 8192)
        put("P2Q", 12288)
        put("LORA", 11776)
        put("WCP_S", 4096)
        put("XRES", 4096)
        put("STATS", 16384)
        assert off <= 206 * KB, off
        a0 = O["YTG"]
        for name, size in [("XSEG", 32768), ("XNB", 16384), ("LATT", 16384),
                           ("WCD", 8192), ("WQUP", 8192), ("WKV", 1024),
                           ("MISC", 8192), ("P1", 16384)]:
            O[name] = a0
            a0 += size
        assert a0 <= O["LORA"], (a0, O["LORA"])
        O["RNB"] = O["K2"]
        O["RT1B"] = O["K2"] + 2048
        O["RT2B"] = O["K2"] + 3072
        self.O = O

    def view(self, off, dt, shape, part0=0, nparts=128):
        esz = {F32: 4, BF16: 2, I32: 4}[dt]
        n = 1
        for s in shape:
            n *= s
        v = self.arena[part0:part0 + nparts, off:off + n * esz].bitcast(dt)
        if len(shape) > 1:
            dims = " ".join(f"d{i}" for i in range(len(shape)))
            kw = {f"d{i}": shape[i] for i in range(1, len(shape))}
            v = v.rearrange(f"p ({dims}) -> p {dims}", **kw)
        return v

    def pbank(self, b, w=512):
        return self.PB[:, b * 512: b * 512 + w]

    def build(self):
        self.alloc()
        O, v = self.O, self.view
        self.c_ident = v(O["CONST"], BF16, [128])
        self.c_tril = v(O["CONST"] + 256, BF16, [128])
        self.c_ones_col = v(O["CONST"] + 512, BF16, [1])
        self.c_ones_row = v(O["CONST"] + 576, BF16, [128], nparts=1)
        self.c_ones_row_f = v(O["CONST"] + 832, F32, [128], nparts=1)
        self.c_bfc = v(O["CONST"] + 1408, F32, [NFC])
        self.c_bpj = v(O["CONST"] + 1600, F32, [NKC])
        self.c_negc = v(O["CONST"] + 1664, F32, [1])
        self.c_rscale_fc = v(O["CONST"] + 1728, F32, [1])
        self.c_scale_fc = v(O["CONST"] + 1792, F32, [1])
        self.c_rscale_pj = v(O["CONST"] + 1856, F32, [1])
        self.c_scale_pj = v(O["CONST"] + 1920, F32, [1])
        self.c_eps = v(O["CONST"] + 1984, F32, [1])
        self.c_nsc_fc = v(O["CONST"] + 2112, F32, [1])
        self.c_nsc_pj = v(O["CONST"] + 2240, F32, [1])
        self.qT = v(O["QT"], BF16, [NKC, T])
        self.kvT = v(O["KVT"], BF16, [T])
        self.k2 = v(O["K2"], BF16, [T], part0=64, nparts=64)
        self.vext = v(O["VEXT"], BF16, [NT, 65])
        self.latT = v(O["LATT"], BF16, [NLC, T])

        self.phase_A()
        self.phase_main()
        self.wait("sync", "dout")

        with ExitStack() as bctx:
            for s in list(self.cnt.keys()):
                self.handles[s] = bctx.enter_context(self.nc.semaphore(s))
            block = bctx.enter_context(self.nc.Block())
            for eng in ENGS:
                def body(e, eng=eng):
                    for fn in self.q[eng]:
                        fn(e)
                getattr(block, eng)(body)
        return self.nc

    def emit_layernorm_seg(self, src, dst, banks, pre_cast_wait=None):
        P, v = self, self.view
        so = self.O["STATS"]
        m_bf = v(so, BF16, [SEG], nparts=1)
        m_f = v(so + KB, F32, [SEG], nparts=1)
        msq = v(so + 3 * KB, F32, [SEG], nparts=1)
        var = v(so + 5 * KB, F32, [SEG], nparts=1)
        stdr = v(so + 7 * KB, F32, [SEG], nparts=1)
        rstd_bf = v(so + 9 * KB, BF16, [SEG], nparts=1)
        xb = [v(so + 10 * KB + b * KB, BF16, [SEG]) for b in range(2)]
        xsq = [v(so + 12 * KB + b * KB, BF16, [SEG]) for b in range(2)]
        tscr = v(so + 14 * KB, F32, [SEG])
        b_sum, b_ssq, b_m, b_r = banks
        ps_sum = self.PB[0:1, b_sum * 512:b_sum * 512 + SEG]
        ps_ssq = self.PB[0:1, b_ssq * 512:b_ssq * 512 + SEG]
        psB_m = self.pbank(b_m)
        psB_r = self.pbank(b_r)
        if pre_cast_wait is not None:
            pre_cast_wait()
        for kc in range(NKC):
            b = kc % 2
            if kc >= 2:
                P.wait("vector", "pe")
                P.wait("scalar", "pe")
            P.inc("vector", "dve", 1, lambda e, kc=kc, b=b: e.tensor_copy(
                xb[b], src[kc]))
            P.inc("scalar", "act", 1, lambda e, kc=kc, b=b: e.activation(
                xsq[b], src[kc], AF.Square))
            P.wait("tensor", "dve")
            P.wait("tensor", "act")
            P.inc("tensor", "pe", 1, P.mm(ps_sum, self.c_ones_col, xb[b],
                                          start=(kc == 0), stop=(kc == NKC - 1)))
            P.inc("tensor", "pe", 1, P.mm(ps_ssq, self.c_ones_col, xsq[b],
                                          start=(kc == 0), stop=(kc == NKC - 1)))
        P.wait("vector", "pe")
        P.inc("vector", "dve", 1, lambda e: e.tensor_scalar(
            m_f, ps_sum, 1.0 / D, None, ALU.mult))
        P.wait("vector", "dve")
        P.inc("vector", "dve", 1, lambda e: e.tensor_copy(m_bf, m_f))
        P.inc("vector", "dve", 1, lambda e: e.tensor_mul(msq, m_f, m_f))
        P.wait("vector", "dve")
        P.inc("vector", "dve", 1, lambda e: e.scalar_tensor_tensor(
            var, ps_ssq, 1.0 / D, msq, ALU.mult, ALU.subtract))
        P.wait("scalar", "dve")
        P.inc("scalar", "act", 1, lambda e: e.activation(
            stdr, var, AF.Ln, bias=self.c_eps[0:1, 0:1]))
        P.wait("scalar", "act")
        P.inc("scalar", "act", 1, lambda e: e.activation(
            rstd_bf, stdr, AF.Exp, scale=-0.5))
        P.wait("tensor", "dve")
        P.wait("tensor", "act")
        P.inc("tensor", "pe", 1, P.mm(psB_m, self.c_ones_row, m_bf))
        P.inc("tensor", "pe", 1, P.mm(psB_r, self.c_ones_row, rstd_bf))
        P.wait("vector", "pe")
        for kc in range(NKC):
            P.inc("vector", "dve", 1, lambda e, kc=kc: e.tensor_sub(
                tscr, src[kc], psB_m))
            P.wait("vector", "dve")
            P.inc("vector", "dve", 1, lambda e, kc=kc: e.tensor_mul(
                dst[kc], tscr, psB_r))

    def phase_A(self):
        P, v, O, din = self, self.view, self.O, self.din
        self.canb = v(O["CONST"] + 2304, BF16, [32])
        P.inc("sync", "dx", 16, lambda e: e.dma_start(self.c_ident, din["ident"]))
        P.inc("sync", "dx", 16, lambda e: e.dma_start(self.c_tril, din["tril"]))
        P.inc("sync", "dx", 16, lambda e: e.dma_start(self.c_bfc, din["bfc2d"]))
        P.inc("sync", "dx", 16, lambda e: e.dma_start(self.c_bpj, din["bpj2d"]))
        w_cd = v(O["WCD"], BF16, [NKC, 512])
        w_qup = v(O["WQUP"], BF16, [NLC, 1024])
        w_kv = v(O["WKV"], BF16, [NLC, 128])
        P.dma_inc("gpsimd", "dw", lambda e: e.dma_start(
            w_cd, din["wcd"].rearrange("(kc p) o -> p kc o", p=128)))
        P.dma_inc("gpsimd", "dw", lambda e: e.dma_start(
            w_qup, din["wqup"].rearrange("(kc p) o -> p kc o", p=128)))
        P.dma_inc("gpsimd", "dw", lambda e: e.dma_start(
            w_kv, din["wkv"].rearrange("(kc p) o -> p kc o", p=128)))
        P.inc("vector", "dve", 1, lambda e: e.memset(self.c_ones_col, 1.0))
        P.inc("vector", "dve", 1, lambda e: e.memset(self.c_ones_row, 1.0))
        P.inc("vector", "dve", 1, lambda e: e.memset(self.c_ones_row_f, 1.0))
        P.inc("vector", "dve", 1, lambda e: e.memset(self.c_eps, 1e-5))

        self.p1buf = [v(O["P1"] + b * 8 * KB, F32, [2048]) for b in range(2)]
        M = O["MISC"]
        self.wfc_acc = v(M + 4 * KB, F32, [1])
        self.wpj_acc = v(M + 4 * KB + 64, F32, [1])
        self.p1tmp = v(M + 4 * KB + 128, F32, [1])
        self.p1_tiles = []
        for kc in range(NKC):
            for half in range(2):
                self.p1_tiles.append(
                    (din["wfcT"][kc * 128:(kc + 1) * 128,
                                 half * 2048:(half + 1) * 2048], 0))
        for c4 in range(16):
            self.p1_tiles.append(
                (din["wpjT"][c4 * 256:(c4 + 1) * 256, :]
                 .rearrange("(a p) o -> p a o", p=128), 1))
        self.p1_dma_ck = {}
        self.p1_red_ck = {}
        self.p1_dma(0)
        self.p1_dma(1)
        p1n = 2

        xseg = [v(O["XSEG"] + b * 16 * KB, F32, [NKC, SEG]) for b in range(2)]
        xnb = [v(O["XNB"] + b * 8 * KB, BF16, [NKC, SEG]) for b in range(2)]

        seg_done_dve = {}
        seg_done_act = {}
        seg_done_pe = {}
        for seg in range(NSEG):
            b = seg % 2
            if seg >= 2:
                P.wait("sync", "dve", seg_done_dve[seg - 2])
                P.wait("sync", "act", seg_done_act[seg - 2])
            for kc in range(NKC):
                P.inc("sync", "dx", 16, lambda e, kc=kc, seg=seg, b=b: e.dma_start(
                    xseg[b][:, kc, :], din["xT"][kc * 128:(kc + 1) * 128,
                                                 seg * SEG:(seg + 1) * SEG]))

            def pre_cast_wait(seg=seg):
                P.wait("vector", "dx")
                P.wait("scalar", "dx")
                if seg >= 2:
                    P.wait("vector", "pe", seg_done_pe[seg - 2])

            src = [xseg[b][:, kc, :] for kc in range(NKC)]
            dst = [xnb[b][:, kc, :] for kc in range(NKC)]
            self.emit_layernorm_seg(src, dst, (0, 1, 2, 3), pre_cast_wait)
            seg_done_dve[seg] = P.ck("dve")
            seg_done_act[seg] = P.ck("act")
            P.wait("tensor", "act")
            P.wait("tensor", "dve")
            for kc in range(NKC):
                for oc in range(NLC):
                    P.inc("tensor", "pe", 1, P.mm(
                        self.pbank(4 + oc),
                        w_cd[:, kc, oc * 128:(oc + 1) * 128],
                        xnb[b][:, kc, :], start=(kc == 0), stop=(kc == NKC - 1)))
            seg_done_pe[seg] = P.ck("pe")
            P.wait("scalar", "pe")
            for oc in range(NLC):
                P.inc("scalar", "act", 1, lambda e, oc=oc, seg=seg: e.activation(
                    self.latT[:, oc, seg * SEG:(seg + 1) * SEG],
                    self.pbank(4 + oc), AF.Copy))
            for _ in range(8):
                self.p1_reduce(p1n - 2)
                self.p1_dma(p1n)
                p1n += 1

        P.wait("tensor", "act")
        for lc in range(NLC):
            for seg in range(NSEG):
                P.inc("tensor", "pe", 1, P.mm(
                    self.pbank(4 + seg), w_kv[:, lc, :],
                    self.latT[:, lc, seg * SEG:(seg + 1) * SEG],
                    start=(lc == 0), stop=(lc == NLC - 1)))
        P.wait("scalar", "pe")
        for seg in range(NSEG):
            P.inc("scalar", "act", 1, lambda e, seg=seg: e.activation(
                self.kvT[:, seg * SEG:(seg + 1) * SEG],
                self.pbank(4 + seg), AF.Copy))
        for oc in range(NKC):
            P.wait("tensor", "act")
            for lc in range(NLC):
                for seg in range(NSEG):
                    P.inc("tensor", "pe", 1, P.mm(
                        self.pbank(seg),
                        w_qup[:, lc, oc * 128:(oc + 1) * 128],
                        self.latT[:, lc, seg * SEG:(seg + 1) * SEG],
                        start=(lc == 0), stop=(lc == NLC - 1)))
            P.wait("scalar", "pe")
            for seg in range(NSEG):
                P.inc("scalar", "act", 1, lambda e, oc=oc, seg=seg: e.activation(
                    self.qT[:, oc, seg * SEG:(seg + 1) * SEG],
                    self.pbank(seg), AF.Copy, scale=0.125))
            if oc % 2 == 0:
                self.p1_reduce(p1n - 2)
                self.p1_dma(p1n)
                p1n += 1

        P.wait("vector", "act")
        P.inc("vector", "dve", 1, lambda e: e.tensor_copy(
            self.k2, self.kvT[0:64, :]))
        P.inc("vector", "dve", 1, lambda e: e.memset(
            v(O["VEXT"], BF16, [NT * 65]), 1.0))
        ps_tr = self.PB[:, 2048:2048 + 32].bitcast(BF16)
        P.wait("tensor", "act")
        for tt in range(NT):
            P.wait("tensor", "dve")
            P.inc("tensor", "pe", 1, lambda e, tt=tt: e.transpose(
                ps_tr, self.kvT[64:128, tt * 128:(tt + 1) * 128],
                self.c_ident[64:128, 64:128]))
            P.wait("vector", "pe")
            P.inc("vector", "dve", 1, lambda e, tt=tt: e.tensor_copy(
                self.vext[:, tt, 0:64], ps_tr))
            if tt % 2 == 0:
                self.p1_reduce(p1n - 2)
                self.p1_dma(p1n)
                p1n += 1
        while p1n - 2 < len(self.p1_tiles):
            self.p1_reduce(p1n - 2)
            self.p1_dma(p1n)
            p1n += 1

        qacc = v(M + 2 * KB, BF16, [1])
        tmp1 = v(M + 2 * KB + 64, BF16, [1])
        r128 = v(M + 2 * KB + 128, BF16, [128], nparts=1)
        scb = v(M + 2 * KB + 512, BF16, [8], nparts=1)
        scf = v(M + 3 * KB, F32, [8], nparts=1)
        P.wait("vector", "act")
        for oc in range(NKC):
            if oc == 0:
                P.inc("vector", "dve", 1, lambda e: e.reduce_max(
                    qacc, self.qT[:, 0, :], axis=AX.X, apply_absolute_value=True))
            else:
                P.inc("vector", "dve", 1, lambda e, oc=oc: e.reduce_max(
                    tmp1, self.qT[:, oc, :], axis=AX.X,
                    apply_absolute_value=True))
                P.wait("vector", "dve")
                P.inc("vector", "dve", 1, lambda e: e.tensor_max(
                    qacc, qacc, tmp1))
        ps_r = self.PB[0:1, 2048 + 256:2048 + 256 + 64].bitcast(BF16)
        P.wait("tensor", "dve")
        P.inc("tensor", "pe", 1, lambda e: e.transpose(ps_r, qacc, self.c_ident))
        P.wait("vector", "pe")
        P.inc("vector", "dve", 1, lambda e: e.tensor_copy(r128, ps_r))
        P.wait("vector", "dve")
        P.inc("vector", "dve", 1, lambda e: e.reduce_max(
            scb[:, 0:1], r128, axis=AX.X))
        P.inc("vector", "dve", 1, lambda e: e.reduce_max(
            tmp1[0:64, :], self.kvT[0:64, :], axis=AX.X,
            apply_absolute_value=True))
        P.wait("tensor", "dve")
        P.inc("tensor", "pe", 1, lambda e: e.transpose(
            ps_r[0:1, 0:64], tmp1[0:64, :], self.c_ident[0:64, 0:64]))
        P.wait("vector", "pe")
        P.inc("vector", "dve", 1, lambda e: e.tensor_copy(
            r128[0:1, 0:64], ps_r[0:1, 0:64]))
        P.wait("vector", "dve")
        P.inc("vector", "dve", 1, lambda e: e.reduce_max(
            scb[:, 1:2], r128[0:1, 0:64], axis=AX.X))
        P.wait("vector", "dve")
        P.inc("vector", "dve", 1, lambda e: e.tensor_mul(
            scf[:, 0:1], scb[:, 0:1], scb[:, 1:2]))
        P.wait("vector", "dve")
        P.inc("vector", "dve", 1, lambda e: e.tensor_scalar(
            scf[:, 1:2], scf[:, 0:1], -64.0, None, ALU.mult))
        P.wait("tensor", "dve")
        P.inc("tensor", "pe", 1, P.mm(self.PB[:, 2048 + 512:2048 + 513],
                                      self.c_ones_row_f, scf[:, 1:2]))
        P.wait("vector", "pe")
        P.inc("vector", "dve", 1, lambda e: e.tensor_copy(
            self.c_negc, self.PB[:, 2048 + 512:2048 + 513]))
        self.negc_ready = P.ck("dve")

        sfs = v(M + 5 * KB, F32, [16], nparts=1)
        P.drain_inc("vector", "dve")
        P.wait("sync", "dve")
        P.dma_inc("sync", "dwf", lambda e: e.dma_start(
            self.scr_acc[0, :], self.wfc_acc))
        P.dma_inc("sync", "dwf", lambda e: e.dma_start(
            self.scr_acc[1, :], self.wpj_acc))
        P.wait("sync", "dwf")
        accrow = v(M + 6 * KB, F32, [2, 128], nparts=1)
        P.dma_inc("sync", "dwf", lambda e: e.dma_start(
            accrow, self.scr_acc.rearrange("a p -> (a p)")))
        acc_row_ck = P.ck("dwf")
        for wi, (rsc, ssc, nsc) in enumerate([
                (self.c_rscale_fc, self.c_scale_fc, self.c_nsc_fc),
                (self.c_rscale_pj, self.c_scale_pj, self.c_nsc_pj)]):
            P.wait("vector", "dwf", acc_row_ck)
            P.inc("vector", "dve", 1, lambda e, wi=wi: e.reduce_max(
                sfs[:, wi * 4:wi * 4 + 1], accrow[:, wi, :], axis=AX.X))
            P.wait("vector", "dve")
            P.inc("vector", "dve", 1, lambda e, wi=wi: e.reciprocal(
                sfs[:, wi * 4 + 1:wi * 4 + 2], sfs[:, wi * 4:wi * 4 + 1]))
            P.wait("vector", "dve")
            P.inc("vector", "dve", 1, lambda e, wi=wi: e.tensor_scalar(
                sfs[:, wi * 4 + 2:wi * 4 + 3], sfs[:, wi * 4 + 1:wi * 4 + 2],
                127.0, None, ALU.mult))
            P.inc("vector", "dve", 1, lambda e, wi=wi: e.tensor_scalar(
                sfs[:, wi * 4 + 3:wi * 4 + 4], sfs[:, wi * 4:wi * 4 + 1],
                -1.0 / 127.0, None, ALU.mult))
            for si, dstv in ((2, rsc), (3, nsc)):
                P.wait("tensor", "dve")
                P.inc("tensor", "pe", 1, P.mm(
                    self.PB[:, 128:129], self.c_ones_row_f,
                    sfs[:, wi * 4 + si:wi * 4 + si + 1]))
                P.wait("vector", "pe")
                P.inc("vector", "dve", 1, lambda e, dstv=dstv: e.tensor_copy(
                    dstv, self.PB[:, 128:129]))
            P.wait("vector", "dve")
            P.inc("vector", "dve", 1, lambda e, ssc=ssc, nsc=nsc: e.tensor_scalar(
                ssc, nsc, -1.0, None, ALU.mult))
        self.scales_ready = P.ck("dve")

        LO = O["LORA"]
        self.afcTb = v(LO, BF16, [NKC, 16])
        self.apjTb = v(LO + 256, BF16, [NFC, 16])
        self.bfcTb = v(LO + 1280, BF16, [FF], nparts=16)
        self.bpjTb = v(LO + 9472, BF16, [D], nparts=16)
        self.rt1b = v(self.O["RT1B"], BF16, [SEG], nparts=16)
        self.rt2b = v(self.O["RT2B"], BF16, [SEG], nparts=16)
        P.wait("gpsimd", "dve", self.scales_ready)
        P.dma_inc("gpsimd", "dw", lambda e: e.dma_start(
            self.afcTb, self.din["afcT"].rearrange("(kc p) r -> p kc r", p=128)))
        P.dma_inc("gpsimd", "dw", lambda e: e.dma_start(
            self.apjTb, self.din["apjT"].rearrange("(kc p) r -> p kc r", p=128)))
        P.dma_inc("gpsimd", "dw", lambda e: e.dma_start(
            self.bfcTb, self.din["bfcT"]))
        P.dma_inc("gpsimd", "dw", lambda e: e.dma_start(
            self.bpjTb, self.din["bpjT"]))
        self.lora_ready = P.ck("dw")

    def p1_dma(self, i):
        P = self
        if i >= len(self.p1_tiles):
            return
        if i >= 2:
            P.wait("sync", "dve", self.p1_red_ck[i - 2])
        src, _ = self.p1_tiles[i]
        dst = self.p1buf[i % 2]
        if len(src.shape) == 3:
            dst = dst.rearrange("p (a o) -> p a o", a=src.shape[1])
        P.dma_inc("sync", "dwf", lambda e, i=i, src=src, dst=dst: e.dma_start(
            dst, src))
        self.p1_dma_ck[i] = P.ck("dwf")

    def p1_reduce(self, i):
        P = self
        if i < 0 or i >= len(self.p1_tiles) or i in self.p1_red_ck:
            return
        _, acc = self.p1_tiles[i]
        accv = self.wfc_acc if acc == 0 else self.wpj_acc
        first = (i == 0) or (acc == 1 and self.p1_tiles[i - 1][1] == 0)
        P.wait("vector", "dwf", self.p1_dma_ck[i])
        if first:
            P.inc("vector", "dve", 1, lambda e, i=i, accv=accv: e.reduce_max(
                accv, self.p1buf[i % 2], axis=AX.X, apply_absolute_value=True))
        else:
            P.inc("vector", "dve", 1, lambda e, i=i: e.reduce_max(
                self.p1tmp, self.p1buf[i % 2], axis=AX.X,
                apply_absolute_value=True))
            P.wait("vector", "dve")
            P.inc("vector", "dve", 1, lambda e, accv=accv: e.tensor_max(
                accv, accv, self.p1tmp))
        self.p1_red_ck[i] = P.ck("dve")

    def setup_p2(self):
        P, v, O = self, self.view, self.O
        PP = O["P2Q"]
        self.p2w = [v(PP + b * 2 * KB, F32, [SEG]) for b in range(2)]
        self.p2i = v(PP + 4 * KB, I32, [SEG])
        self.p2u = v(PP + 4 * KB, F32, [SEG])
        self.p2f = v(PP + 6 * KB, F32, [SEG])
        self.p2r = v(PP + 8 * KB, F32, [SEG])
        self.p2s = [v(PP + 10 * KB + b * KB, BF16, [SEG]) for b in range(2)]
        self.p2chk = v(self.O["CONST"] + 2304, BF16, [8])
        self.p2_tiles = []
        for cb in range(8):
            for kc in range(NKC):
                self.p2_tiles.append(("fc", kc, cb))
        for cb2 in range(2):
            for kc2 in range(NFC):
                self.p2_tiles.append(("pj", kc2, cb2))
        self.p2_next = 0
        self.p2_dma_ck = {}
        self.p2_done = {}
        self.p2_scr_ck = {}

    def p2_tile(self):
        P, din = self, self.din
        i = self.p2_next
        if i >= len(self.p2_tiles):
            return False
        self.p2_next += 1
        kind, kc, cb = self.p2_tiles[i]
        src = (din["wfcT"] if kind == "fc" else din["wpjT"])[
            kc * 128:(kc + 1) * 128, cb * 512:(cb + 1) * 512]
        dstd = (self.scr_fc if kind == "fc" else self.scr_pj)[cb, :, kc, :]
        if i >= 2:
            P.wait("sync", "dve", self.p2_done[i - 2])
        P.dma_inc("sync", "dwf", lambda e, i=i, src=src: e.dma_start(
            self.p2w[i % 2], src))
        self.p2_dma_ck[i] = P.ck("dwf")
        rsc = self.c_rscale_fc if kind == "fc" else self.c_rscale_pj
        nsc = self.c_nsc_fc if kind == "fc" else self.c_nsc_pj
        P.wait("vector", "dwf", self.p2_dma_ck[i])
        if i == 0:
            P.wait("vector", "dve", self.scales_ready)
        P.inc("vector", "dve", 1, lambda e, i=i, rsc=rsc: e.tensor_scalar(
            self.p2r, self.p2w[i % 2], rsc[:, 0:1], 256.5, ALU.mult, ALU.add))
        P.wait("vector", "dve")
        P.inc("vector", "dve", 1, lambda e: e.tensor_copy(self.p2i, self.p2r))
        P.wait("vector", "dve")
        P.inc("vector", "dve", 1, lambda e: e.tensor_copy(self.p2f, self.p2i))
        P.wait("vector", "dve")
        P.inc("vector", "dve", 1, lambda e: e.tensor_sub(
            self.p2r, self.p2r, self.p2f))
        P.wait("vector", "dve")
        P.inc("vector", "dve", 1, lambda e: e.scalar_tensor_tensor(
            self.p2u, self.p2r, 0.0, self.p2f, ALU.is_lt, ALU.subtract))
        P.wait("vector", "dve")
        if i >= 2:
            P.wait("vector", "dscr", 16 * (i - 1))
        P.inc("vector", "dve", 1, lambda e, i=i, nsc=nsc: e.tensor_scalar(
            self.p2s[i % 2], self.p2u, 256.0, nsc[:, 0:1], ALU.add, ALU.mult))
        P.drain_inc("vector", "dve")
        self.p2_done[i] = P.ck("dve")
        P.wait("gpsimd", "dve", self.p2_done[i])
        P.dma_inc("gpsimd", "dscr", lambda e, i=i, dstd=dstd: e.dma_start(
            dstd, self.p2s[i % 2]))
        self.p2_scr_ck[i] = P.ck("dscr")
        return True

    def p2_fc_ready(self, cb):
        while cb * 8 + 7 not in self.p2_scr_ck:
            self.p2_tile()
        return self.p2_scr_ck[cb * 8 + 7]

    def p2_pj_ready(self, cb2, kh4):
        while 64 + cb2 * 32 + kh4 * 8 + 7 not in self.p2_scr_ck:
            self.p2_tile()
        return self.p2_scr_ck[64 + cb2 * 32 + kh4 * 8 + 7]

    def enqueue_ffn(self, g):
        P = self
        xn2 = self.xn2g[g % 2]
        x1 = self.x1g[g % 2]
        hT = self.hT
        gcols = slice(g * SEG, (g + 1) * SEG)
        xn2_ck = self.xn2_ck[g]
        x1_ck = self.x1_ck[g]
        bgq = self.bgq
        rt_ps = self.pbank(7)
        wfc_t = self.wfc_t
        wpj_t = self.wpj_t

        def c_rt1():
            P.wait("tensor", "dw", self.lora_ready)
            P.wait("tensor", "dve", xn2_ck)
            P.wait("tensor", "dve",
                   self.fin_ck.get((self.gprev[g], 1, 1), 0))
            P.wait("tensor", "act",
                   self.rt2_evac_ck.get(self.gprev[g], 0))
            for kc in range(NKC):
                P.inc("tensor", "pe", 1, P.mm(
                    rt_ps[0:16, :], self.afcTb[:, kc, :], xn2[:, kc, :],
                    start=(kc == 0), stop=(kc == NKC - 1)))
            ck = P.ck("pe")
            P.wait("scalar", "pe", ck)
            P.inc("scalar", "act", 1, lambda e: e.activation(
                self.rt1b, rt_ps[0:16, :], AF.Copy, scale=2.0))
            self.rt1_evac_ck[g] = P.ck("act")
        bgq.append(c_rt1)
        self.n_queued += 1

        for ocg in range(8):
            def c_wfc(ocg=ocg):
                i = self.wfc_i
                self.wfc_i += 1
                P.wait("gpsimd", "dscr", self.p2_fc_ready(ocg))
                if i >= 2:
                    P.wait("gpsimd", "pe", self.wfc_used_ck[i - 2])
                P.dma_inc("gpsimd", "dffn", lambda e, ocg=ocg, i=i: e.dma_start(
                    wfc_t[i % 2], self.scr_fc[ocg]))
                self.wfc_dma_ck[i] = P.ck("dffn")
            bgq.append(c_wfc)
            self.n_queued += 1
            for jj in range(4):
                def c_fc(ocg=ocg, jj=jj):
                    i = self.wfc_i - 1
                    oc = ocg * 4 + jj
                    pb = self.pbank(6 + oc % 2)
                    P.wait("tensor", "dffn", self.wfc_dma_ck[i])
                    if oc >= 2:
                        P.wait("tensor", "act", self.gelu_ck[(g, oc - 2)])
                    elif oc == 1:
                        P.wait("tensor", "act", self.rt1_evac_ck[g])
                    else:
                        P.wait("tensor", "act",
                               self.gelu_ck.get((self.gprev[g], 30), 0))
                    for kc in range(NKC):
                        P.inc("tensor", "pe", 1, P.mm(
                            pb, wfc_t[i % 2][:, kc, jj * 128:(jj + 1) * 128],
                            xn2[:, kc, :], start=(kc == 0), stop=False))
                    P.inc("tensor", "pe", 1, P.mm(
                        pb, self.bfcTb[:, oc * 128:(oc + 1) * 128],
                        self.rt1b, start=False, stop=True))
                    mmck = P.ck("pe")
                    if jj == 3:
                        self.wfc_used_ck[i] = mmck
                    P.wait("scalar", "pe", mmck)
                    P.inc("scalar", "act", 1,
                          lambda e, oc=oc, pb=pb: e.activation(
                              hT[:, oc, :], pb, AF.Gelu,
                              bias=self.c_bfc[:, oc:oc + 1]))
                    self.gelu_ck[(g, oc)] = P.ck("act")
                bgq.append(c_fc)
                self.n_queued += 1
        self.fc_end_q[g] = self.n_queued

        def c_rt2():
            P.wait("tensor", "act", self.gelu_ck[(g, 31)])
            for kc2 in range(NFC):
                P.inc("tensor", "pe", 1, P.mm(
                    rt_ps[0:16, :], self.apjTb[:, kc2, :], hT[:, kc2, :],
                    start=(kc2 == 0), stop=(kc2 == NFC - 1)))
            ck = P.ck("pe")
            P.wait("scalar", "pe", ck)
            P.inc("scalar", "act", 1, lambda e: e.activation(
                self.rt2b, rt_ps[0:16, :], AF.Copy, scale=2.0))
            self.rt2_evac_ck[g] = P.ck("act")
        bgq.append(c_rt2)
        self.n_queued += 1

        for cb2 in range(2):
            for jp in range(2):
                for kh4 in range(4):
                    def c_wpj(cb2=cb2, jp=jp, kh4=kh4):
                        P.wait("gpsimd", "dscr", self.p2_pj_ready(cb2, kh4))
                        P.wait("gpsimd", "pe")
                        P.dma_inc("gpsimd", "dffn",
                              lambda e, cb2=cb2, kh4=kh4: e.dma_start(
                                  wpj_t, self.scr_pj[cb2, :,
                                                     kh4 * 8:(kh4 + 1) * 8, :]))
                    bgq.append(c_wpj)
                    self.n_queued += 1

                    def c_pj(cb2=cb2, jp=jp, kh4=kh4):
                        P.wait("tensor", "dffn")
                        if kh4 == 0:
                            P.wait("tensor", "act", self.rt2_evac_ck[g])
                            prev = (g, cb2, jp - 1) if jp else (g, cb2 - 1, 1)
                            P.wait("tensor", "dve",
                                   self.fin_ck.get(prev, 0))
                        for k8 in range(8):
                            kc2 = kh4 * 8 + k8
                            for jh in range(2):
                                pb = self.pbank(6 + jh)
                                P.inc("tensor", "pe", 1, P.mm(
                                    pb, wpj_t[:, k8,
                                              (jp * 2 + jh) * 128:
                                              (jp * 2 + jh) * 128 + 128],
                                    hT[:, kc2, :],
                                    start=(kc2 == 0), stop=False))
                    bgq.append(c_pj)
                    self.n_queued += 1

                def c_pj_fin(cb2=cb2, jp=jp, gcols=gcols, x1=x1, x1_ck=x1_ck):
                    for jh in range(2):
                        oc2 = cb2 * 4 + jp * 2 + jh
                        pb = self.pbank(6 + jh)
                        P.inc("tensor", "pe", 1, P.mm(
                            pb, self.bpjTb[:, oc2 * 128:(oc2 + 1) * 128],
                            self.rt2b, start=False, stop=True))
                    mmck = P.ck("pe")
                    P.wait("vector", "pe", mmck)
                    P.wait("vector", "dve", x1_ck)
                    for jh in range(2):
                        oc2 = cb2 * 4 + jp * 2 + jh
                        pb = self.pbank(6 + jh)
                        P.inc("vector", "dve", 1,
                              lambda e, oc2=oc2, pb=pb: e.scalar_tensor_tensor(
                                  x1[:, oc2, :], pb,
                                  self.c_bpj[:, oc2:oc2 + 1],
                                  x1[:, oc2, :], ALU.add, ALU.add))
                    P.drain_inc("vector", "dve")
                    dck = P.ck("dve")
                    P.wait("sync", "dve", dck)
                    for jh in range(2):
                        oc2 = cb2 * 4 + jp * 2 + jh
                        P.dma_inc("sync", "dout",
                              lambda e, oc2=oc2, gcols=gcols: e.dma_start(
                                  self.outT[oc2 * 128:(oc2 + 1) * 128, gcols],
                                  x1[:, oc2, :]))
                    self.fin_ck[(g, cb2, jp)] = P.ck("dve")
                bgq.append(c_pj_fin)
                self.n_queued += 1
        self.ffn_end_q[g] = self.n_queued

    def pump(self, n):
        for _ in range(n):
            if not self.bgq:
                return
            self.n_pumped += 1
            self.bgq.popleft()()

    def pump_until(self, target):
        while self.n_pumped < target and self.bgq:
            self.pump(1)

    def phase_main(self):
        P, v, O = self, self.view, self.O
        self.bgq = collections.deque()
        self.n_pumped = 0
        self.n_queued = 0
        self.fc_end_q = {}
        self.setup_p2()
        self.wfc_t = [v(O["WFC_D"] + b * 8 * KB, BF16, [NKC, 512])
                      for b in range(2)]
        self.wpj_t = v(O["WPJ_D"], BF16, [8, 512])
        self.wfc_i = 0
        self.wfc_dma_ck = {}
        self.wfc_used_ck = {}
        self.hT = v(O["HT"], BF16, [NFC, SEG])
        self.x1g = [v(O["X1G"] + b * 16 * KB, F32, [NKC, SEG]) for b in range(2)]
        self.xn2g = [v(O["XN2G"], BF16, [NKC, SEG])] * 2
        self.ytg = v(O["YTG"], BF16, [NKC, SEG])
        self.xn2_ck = {}
        self.x1_ck = {}
        self.gelu_ck = {}
        self.rt1_evac_ck = {}
        self.rt2_evac_ck = {}
        self.fin_ck = {}
        self.ffn_end_q = {}
        self.wcp_used_ck = {}
        wcp = [v(O["WCP_S"] + b * 2 * KB, BF16, [NKC, 128]) for b in range(2)]
        xres = [v(O["XRES"] + b * 2 * KB, F32, [SEG]) for b in range(2)]
        rnB = v(O["RNB"], F32, [SEG], nparts=64)
        rn_row = v(O["STATS"], F32, [1024], nparts=1)
        rn_bf = v(O["STATS"] + 4 * KB, BF16, [T], nparts=1)
        attT = [v(O["ATT"] + b * 2 * KB, BF16, [1024]) for b in range(2)]
        sc_ps = self.PB[:, 0:1024]
        yext = self.PB[:, 1024:3072]

        def q_mov(qt, par, cg):
            base = 64 * par
            return self.qT[base:base + 64, cg * 4:(cg + 1) * 4,
                           qt * 128:(qt + 1) * 128]

        exp_ck = {}
        av_ck = {}
        mask_ck = {}
        yev_ck = 0
        j = 0
        pend = []

        def emit_av(item):
            jj, qt, sc, hg, buf = item
            P.wait("tensor", "act", exp_ck[jj])
            if sc == qt:
                P.wait("tensor", "pool", mask_ck[jj])
            P.inc("tensor", "pe", 1, P.mm(
                yext[0:65, hg * 1024: hg * 1024 + 512],
                self.vext[:, sc, :], attT[buf][:, 0:512],
                start=(sc == 0), stop=(sc == qt)))
            av_ck[jj] = P.inc("tensor", "pe", 1, P.mm(
                yext[0:65, hg * 1024 + 512: hg * 1024 + 1024],
                self.vext[:, sc, :], attT[buf][:, 512:1024],
                start=(sc == 0), stop=(sc == qt)))

        pace = [10, 12, 14, 16, 12, 12, 12, 12, 10, 8, 6, 4, 0, 0, 0, 0]
        order = [3, 2, 1, 0]
        self.gprev = {}
        prev = None
        for g in order:
            self.gprev[g] = prev
            prev = g
        pos = 0
        first_qt = True
        for g in order:
            grp_start_dve = P.ck("dve")
            for qi, qt in enumerate(range(4 * g, 4 * g + 4)):
                for _ in range(pace[pos]):
                    if not self.p2_tile():
                        break
                pos += 1
                for sc in range(qt + 1):
                    for hg in range(2):
                        buf = j % 2
                        if j >= 1:
                            P.wait("tensor", "act", exp_ck[j - 1])
                        if sc == 0 and hg == 0:
                            if qi == 0:
                                P.wait("tensor", "dve", grp_start_dve)
                            if not first_qt:
                                P.wait("tensor", "dve", yev_ck)
                        lhsK = self.kvT[0:64, sc * 128:(sc + 1) * 128]
                        P.inc("tensor", "pe", 1, P.mm(
                            sc_ps[:, 0:512], lhsK, q_mov(qt, 0, hg),
                            tp=(0, 0)))
                        qk_ck = P.inc("tensor", "pe", 1, P.mm(
                            sc_ps[:, 512:1024],
                            self.k2[:, sc * 128:(sc + 1) * 128],
                            q_mov(qt, 1, hg), tp=(64, 0)))
                        if pend:
                            emit_av(pend.pop(0))
                        P.wait("scalar", "pe", qk_ck)
                        if j >= 2:
                            P.wait("scalar", "pe", av_ck.get(j - 2, 0))
                        if j == 0:
                            P.wait("scalar", "dve", self.negc_ready)
                        P.inc("scalar", "act", 1, lambda e, buf=buf: e.activation(
                            attT[buf], sc_ps, AF.Exp, bias=self.c_negc[:, 0:1]))
                        exp_ck[j] = P.ck("act")
                        if sc == qt:
                            P.wait("gpsimd", "act", exp_ck[j])
                            for jj in range(8):
                                P.inc("gpsimd", "pool", 1,
                                      lambda e, buf=buf, jj=jj: e.tensor_mul(
                                          attT[buf][:, jj * 128:(jj + 1) * 128],
                                          attT[buf][:, jj * 128:(jj + 1) * 128],
                                          self.c_tril))
                            mask_ck[j] = P.ck("pool")
                        pend.append((j, qt, sc, hg, buf))
                        self.pump(1)
                        j += 1
                while pend:
                    emit_av(pend.pop(0))
                P.wait("scalar", "pe", av_ck[j - 1])
                for hh in range(2):
                    P.inc("scalar", "act", 1, lambda e, hh=hh: e.activation(
                        rn_row, yext[64:65, hh * 1024:(hh + 1) * 1024], AF.Ln))
                    P.wait("scalar", "act")
                    P.inc("scalar", "act", 1, lambda e, hh=hh: e.activation(
                        rn_bf[0:1, hh * 1024:(hh + 1) * 1024], rn_row,
                        AF.Exp, scale=-1.0))
                P.wait("tensor", "act")
                for half in range(2):
                    for q2 in range(2):
                        c0 = half * 1024 + q2 * 512
                        P.inc("tensor", "pe", 1, P.mm(
                            sc_ps[0:64, q2 * 512:q2 * 512 + 512],
                            self.c_ones_row[0:1, 0:64],
                            rn_bf[0:1, c0:c0 + 512]))
                    P.wait("vector", "pe")
                    for par in range(2):
                        cols0 = half * 1024 + par * 512
                        dst = self.ytg[par * 64:par * 64 + 64,
                                       half * 4:(half + 1) * 4,
                                       (qt % 4) * 128:(qt % 4) * 128 + 128]
                        P.inc("vector", "dve", 1,
                              lambda e, par=par: e.tensor_copy(
                                  rnB, sc_ps[0:64, par * 512:par * 512 + 512]))
                        P.wait("vector", "dve")
                        P.inc("vector", "dve", 1,
                              lambda e, dst=dst, cols0=cols0: e.tensor_mul(
                                  dst, yext[0:64, cols0:cols0 + 512], rnB))
                    if half == 0:
                        P.wait("tensor", "dve")
                yev_ck = P.ck("dve")
                first_qt = False
                self.pump(3)
            self.group_tail(g, wcp, xres)
        while self.bgq:
            self.bgq.popleft()()

        if self.debug:
            P.wait("sync", "act")
            P.wait("sync", "dve")
            dbg = self.dbg
            P.dma_inc("sync", "dout", lambda e: e.dma_start(
                dbg["d_qT"], v(O["QT"], BF16, [NKC * T])))
            P.dma_inc("sync", "dout", lambda e: e.dma_start(
                dbg["d_kvT"], self.kvT))
            P.dma_inc("sync", "dout", lambda e: e.dma_start(
                dbg["d_negc"], self.c_negc))
            P.wait("sync", "dscr")
            for cb in range(8):
                P.dma_inc("sync", "dout", lambda e, cb=cb: e.dma_start(
                    dbg["d_wfc"][cb], self.scr_fc[cb]))
            for cb in range(2):
                P.dma_inc("sync", "dout", lambda e, cb=cb: e.dma_start(
                    dbg["d_wpj"][cb], self.scr_pj[cb]))
            P.wait("sync", "pe")
            P.wait("sync", "pool")
            P.dma_inc("sync", "dout", lambda e: e.dma_start(
                dbg["d_ytg"], v(O["YTG"], BF16, [NKC * SEG])))
            P.dma_inc("sync", "dout", lambda e: e.dma_start(
                dbg["d_x1g0"], v(O["X1G"], F32, [NKC * SEG])))
            P.dma_inc("sync", "dout", lambda e: e.dma_start(
                dbg["d_x1g1"], v(O["X1G"] + 16 * KB, F32, [NKC * SEG])))
            P.dma_inc("sync", "dout", lambda e: e.dma_start(
                dbg["d_xn2"], v(O["XN2G"], BF16, [NKC * SEG])))
            P.dma_inc("sync", "dout", lambda e: e.dma_start(
                dbg["d_hT"], v(O["HT"], BF16, [NFC * SEG])))
            P.dma_inc("sync", "dout", lambda e: e.dma_start(
                dbg["d_rt"], v(O["RT1B"], BF16, [2 * SEG], nparts=16)))
            P.dma_inc("sync", "dout", lambda e: e.dma_start(
                dbg["d_rnbf"], v(O["STATS"] + 4 * KB, BF16, [T], nparts=1)))
            P.dma_inc("sync", "dout", lambda e: e.dma_start(
                dbg["d_const"], v(O["CONST"], F32, [1024])))
            P.dma_inc("sync", "dout", lambda e: e.dma_start(
                dbg["d_acc"], v(O["MISC"] + 6 * KB, F32, [256], nparts=1)))

    def group_tail(self, g, wcp, xres):
        P, v, O, din = self, self.view, self.O, self.din
        gcols = slice(g * SEG, (g + 1) * SEG)
        x1 = self.x1g[g % 2]
        xn2 = self.xn2g[g % 2]
        P.wait("tensor", "dve")
        gp = self.gprev[g]
        gpp = self.gprev[gp] if gp is not None else None
        if gpp is not None:
            self.pump_until(self.ffn_end_q[gpp])
        prev_add_ck = {}
        for oc in range(NKC):
            if oc >= 2:
                P.wait("gpsimd", "pe", self.wcp_used_ck[(g, oc - 2)])
            P.dma_inc("gpsimd", "dw", lambda e, oc=oc: e.dma_start(
                wcp[oc % 2],
                din["wcproj"].rearrange("(kc p) o -> p kc o", p=128)
                [:, :, oc * 128:(oc + 1) * 128]))
            wcp_ck = P.ck("dw")
            if oc >= 2:
                P.wait("sync", "dve", prev_add_ck[oc - 2])
            P.dma_inc("sync", "dres", lambda e, oc=oc, gcols=gcols: e.dma_start(
                xres[oc % 2], din["xT"][oc * 128:(oc + 1) * 128, gcols]))
            xr_ck = P.ck("dres")
            pb = self.pbank(oc % 2)
            P.wait("tensor", "dw", wcp_ck)
            if oc >= 2:
                P.wait("tensor", "dve", prev_add_ck[oc - 2])
            for kc in range(NKC):
                P.inc("tensor", "pe", 1, P.mm(
                    pb, wcp[oc % 2][:, kc, :], self.ytg[:, kc, :],
                    start=(kc == 0), stop=(kc == NKC - 1)))
            self.wcp_used_ck[(g, oc)] = P.ck("pe")
            P.wait("vector", "pe")
            P.wait("vector", "dres", xr_ck)
            P.wait("vector", "dout")
            P.inc("vector", "dve", 1, lambda e, oc=oc, x1=x1, pb=pb: e.tensor_add(
                x1[:, oc, :], pb, xres[oc % 2]))
            prev_add_ck[oc] = P.ck("dve")
            self.pump(1)
        self.x1_ck[g] = P.ck("dve")
        src = [x1[:, kc, :] for kc in range(NKC)]
        dst = [xn2[:, kc, :] for kc in range(NKC)]

        if gp is not None:
            self.pump_until(self.fc_end_q[gp])

        def ln2_wait():
            P.wait("scalar", "dve", self.x1_ck[g])
            if gp is not None:
                P.wait("vector", "pe")
                P.wait("scalar", "pe")

        self.emit_layernorm_seg(src, dst, (0, 1, 0, 1), ln2_wait)
        self.xn2_ck[g] = P.ck("dve")
        self.enqueue_ffn(g)


def build_program(debug=False):
    b = Builder(debug=debug)
    nc = b.build()
    return b, nc


def kernel(L0_attn_x, L0_attn_Wc_down, L0_attn_Wq_up, L0_attn_Wkv_up,
           L0_attn_Wc_proj, L0_ffn_W_fc, L0_ffn_b_fc, L0_ffn_A_fc,
           L0_ffn_B_fc, L0_ffn_W_proj, L0_ffn_b_proj, L0_ffn_A_proj,
           L0_ffn_B_proj):
    b, nc = build_program(debug=False)

    bf = ml_dtypes.bfloat16
    shared = {
        "wcd": np.ascontiguousarray(L0_attn_Wc_down, dtype=np.float32),
        "wqup": np.ascontiguousarray(L0_attn_Wq_up, dtype=np.float32),
        "wkv": np.ascontiguousarray(L0_attn_Wkv_up, dtype=np.float32),
        "wcproj": np.ascontiguousarray(L0_attn_Wc_proj, dtype=np.float32),
        "wfcT": np.ascontiguousarray(np.asarray(L0_ffn_W_fc).T, dtype=np.float32),
        "wpjT": np.ascontiguousarray(np.asarray(L0_ffn_W_proj).T, dtype=np.float32),
        "afcT": np.ascontiguousarray(np.asarray(L0_ffn_A_fc).T, dtype=np.float32),
        "bfcT": np.ascontiguousarray(np.asarray(L0_ffn_B_fc).T, dtype=np.float32),
        "apjT": np.ascontiguousarray(np.asarray(L0_ffn_A_proj).T, dtype=np.float32),
        "bpjT": np.ascontiguousarray(np.asarray(L0_ffn_B_proj).T, dtype=np.float32),
        "bfc2d": np.ascontiguousarray(
            np.asarray(L0_ffn_b_fc).reshape(NFC, 128).T, dtype=np.float32),
        "bpj2d": np.ascontiguousarray(
            np.asarray(L0_ffn_b_proj).reshape(NKC, 128).T, dtype=np.float32),
        "ident": np.eye(128, dtype=bf),
        "tril": (np.tril(np.ones((128, 128), dtype=np.float32)).T).astype(bf),
    }
    x = np.asarray(L0_attn_x)
    in_maps = []
    for c in range(8):
        m = dict(shared)
        m["xT"] = np.ascontiguousarray(x[c].T, dtype=np.float32)
        in_maps.append(m)
    res = run_bass_kernel_spmd(nc, in_maps, core_ids=list(range(8)))
    outs = []
    for c in range(8):
        o = np.asarray(res.results[c]["outT"])
        outs.append(np.ascontiguousarray(o.T))
    return np.stack(outs, axis=0)
